# version 04
# baseline (speedup 1.0000x reference)
import numpy as np

import concourse.bass as bass
import concourse.mybir as mybir
from concourse.bass_utils import run_bass_kernel_spmd
from concourse.libnrt import physical_to_logical_core_id

F32 = mybir.dt.float32
BF16 = mybir.dt.bfloat16
I32 = mybir.dt.int32
AF = mybir.ActivationFunctionType
ALU = mybir.AluOpType
AX = mybir.AxisListType

D = 2048
HID = 8192
SEQ = 8192
NH = 16
TPC = 1024
COV = 4
TB = 512
DC = 16
HC = 64
NMAGIC = 4
INVSQHD = float(1.0 / np.sqrt(128.0))
EPS = 1e-5
HGROUPS = [(0, 6), (6, 12), (12, 16)]


class Sem:
    def __init__(self, handle):
        self.h = handle
        self.v = 0


class Rec:

    def __init__(self, script, name):
        self.__dict__["_s"] = script
        self.__dict__["_n"] = name

    def __getattr__(self, op):
        def rec(*a, inc=None, **k):
            pair = None
            if inc is not None:
                sem, n = inc
                sem.v += n
                pair = (sem.h, n)
            self._s.append((self._n, op, a, k, pair))
        return rec

    def wait(self, sem, v=None):
        t = sem.v if v is None else v
        if t > 0:
            self._s.append((self._n, "wait_ge", (sem.h, t), {}, None))

    def raw(self, fn):
        self._s.append((self._n, None, (fn,), {}, None))


def build_program():
    nc = bass.Bass("TRN2", target_bir_lowering=False)

    xcov_h = nc.dram_tensor("xcov", [COV, 128, DC, TPC], F32, kind="ExternalInput").ap()
    wq_h = nc.dram_tensor("wqt", [DC, 128, DC, 128], F32, kind="ExternalInput").ap()
    wk_h = nc.dram_tensor("wkt", [DC, 128, DC, 128], F32, kind="ExternalInput").ap()
    wv_h = nc.dram_tensor("wvt", [DC, 128, DC, 128], F32, kind="ExternalInput").ap()
    wo_h = nc.dram_tensor("wot", [DC, 128, DC, 128], F32, kind="ExternalInput").ap()
    up_h = nc.dram_tensor("upt", [HC, 128, DC, 128], F32, kind="ExternalInput").ap()
    dn_h = nc.dram_tensor("dnt", [8, 128, HC, 256], F32, kind="ExternalInput").ap()
    upsc_h = nc.dram_tensor("upsc", [128, HC], F32, kind="ExternalInput").ap()
    dnsc_h = nc.dram_tensor("dnsc", [128, DC], F32, kind="ExternalInput").ap()
    osc_h = nc.dram_tensor("osc", [1, 1], F32, kind="ExternalInput").ap()
    cscov_h = nc.dram_tensor("cscov", [COV, 2, 128, TPC], BF16, kind="ExternalInput").ap()
    csown_h = nc.dram_tensor("csown", [2, 128, TPC], BF16, kind="ExternalInput").ap()
    cf_h = nc.dram_tensor("constf", [128, 648], F32, kind="ExternalInput").ap()
    magic_h = nc.dram_tensor("magic", [1, NMAGIC], I32, kind="ExternalInput").ap()
    out_h = nc.dram_tensor("out", [128, DC, TPC], F32, kind="ExternalOutput").ap()

    upsgn_h = nc.dram_tensor("upsgn", [HC, 128, DC, 128], BF16, kind="Internal").ap()
    dnsgn_h = nc.dram_tensor("dnsgn", [8, 128, HC, 256], BF16, kind="Internal").ap()
    wkbf_h = nc.dram_tensor("wkbf", [DC, 128, DC, 128], BF16, kind="Internal").ap()
    wvbf_h = nc.dram_tensor("wvbf", [DC, 128, DC, 128], BF16, kind="Internal").ap()
    xmid_h = nc.dram_tensor("xmidscr", [COV, 128, DC, TPC], F32, kind="Internal").ap()
    x1_h = nc.dram_tensor("x1scr", [128, DC, TPC], F32, kind="Internal").ap()

    shk_h = nc.dram_tensor("shk", [128, NH, SEQ], BF16, kind="Internal", addr_space="Shared").ap()
    shv_h = nc.dram_tensor("shv", [SEQ, D], BF16, kind="Internal", addr_space="Shared").ap()
    shkn_h = nc.dram_tensor("shkn", [2, 16], F32, kind="Internal", addr_space="Shared").ap()
    shfl_h = nc.dram_tensor("shfl", [2, NMAGIC], I32, kind="Internal", addr_space="Shared").ap()

    ctx = []

    def enter(t):
        ctx.append(t)
        return t.__enter__()

    sbuf = lambda n, sh, dt: enter(nc.sbuf_tensor(n, sh, dt))
    psum = lambda n: enter(nc.psum_tensor(n, [128, 512], F32))

    XB = sbuf("XB", [128, DC, TB], F32)
    LNB = sbuf("LNB", [128, DC, TPC], BF16)
    KRB = sbuf("KRB", [128, NH, TPC], BF16)
    VTB = sbuf("VTB", [128, 8, D], BF16)
    WF = sbuf("WF", [128, 2048], F32)
    WFB = sbuf("WFB", [128, 4, 2048], BF16)
    KSL = sbuf("KSL", [128, 2, 6, 256], BF16)
    VSL = sbuf("VSL", [128, 2, 768], BF16)
    ACC = sbuf("ACC", [128, 6, TB], F32)
    EXB = sbuf("EXB", [128, 4, TB], BF16)
    KPRE = EXB[:, 2, :]
    KSQ = EXB[:, 0:2, :].rearrange("p a b -> p (a b)")
    T1 = EXB[:, 0, :]
    T2 = EXB[:, 1, :]
    TF = sbuf("TF", [128, 2, TB], F32)
    TD = sbuf("TD", [128, TB], F32)
    RSB = sbuf("RSB", [128, TB], F32)
    MRSB = sbuf("MRSB", [128, TB], F32)
    CSB = sbuf("CSB", [128, 2, TPC], BF16)
    XBF = sbuf("XBF", [128, 2, TB], BF16)
    XSQ = sbuf("XSQ", [128, 2, TB], BF16)
    STAT = sbuf("STAT", [1, TB, 3], F32)
    PRM = sbuf("PRM", [128, 128], BF16)
    IDB = sbuf("IDB", [128, 128], BF16)
    IDN = sbuf("IDN", [128, 128], F32)
    ONEC = sbuf("ONEC", [128, 1], BF16)
    ROWS = sbuf("ROWS", [1, 3, 128], F32)
    UPS = sbuf("UPS", [128, HC], F32)
    DNS = sbuf("DNS", [128, DC], F32)
    CMB = sbuf("CMB", [128, DC], F32)
    OSS = sbuf("OSS", [1, 1], F32)
    NRM = sbuf("NRM", [1, 4, 16], F32)
    KNT = sbuf("KNT", [1, 2], F32)
    NEGM = sbuf("NEGM", [128, 16], F32)
    EPSC = sbuf("EPSC", [128, 1], F32)
    MGC = sbuf("MGC", [1, NMAGIC], I32)
    ZRO = sbuf("ZRO", [1, NMAGIC], I32)
    PKN = sbuf("PKN", [1, 16], F32)

    P = [psum(f"pp{i}") for i in range(8)]
    ATN = VTB[:, :, :].rearrange("p a b -> p (a b)")

    sems = {}

    def mksem(n):
        s = Sem(enter(nc.semaphore(n)))
        sems[n] = s
        return s

    dIO = mksem("dIO")
    dSH = mksem("dSH")
    dKV = mksem("dKV")
    dWS = mksem("dWS")
    dWG = mksem("dWG")
    tS = mksem("tS")
    vS = mksem("vS")
    cS = mksem("cS")
    pS = mksem("pS")

    script = []
    TE = Rec(script, "tensor")
    VE = Rec(script, "vector")
    SE = Rec(script, "scalar")
    GE = Rec(script, "gpsimd")
    SY = Rec(script, "sync")

    bguard = {}
    wfb_busy = {0: (tS, 0), 1: (tS, 0), 2: (tS, 0), 3: (tS, 0)}

    def pwait(p):
        g = bguard.pop(id(p), None)
        if g is not None:
            TE.wait(g[0], g[1])

    def pdone(p, sem):
        bguard[id(p)] = (sem, sem.v)

    GE.dma_start(out=PRM[:, :], in_=cf_h[:, 0:128], inc=(dWG, 16))
    GE.dma_start(out=IDB[:, :], in_=cf_h[:, 128:256], inc=(dWG, 16))
    GE.dma_start(out=IDN[:, :], in_=cf_h[:, 128:256], inc=(dWG, 16))
    GE.dma_start(out=ONEC[:, :], in_=cf_h[:, 256:257], allow_slow_non_contiguous=True, inc=(dWG, 16))
    GE.dma_start(out=ROWS[:, 0, :], in_=cf_h[0:1, 257:385], inc=(dWG, 16))
    GE.dma_start(out=ROWS[:, 1, :], in_=cf_h[0:1, 385:513], inc=(dWG, 16))
    GE.dma_start(out=ROWS[:, 2, :], in_=cf_h[0:1, 513:641], inc=(dWG, 16))
    GE.dma_start(out=UPS[:, :], in_=upsc_h[:, :], inc=(dWG, 16))
    GE.dma_start(out=DNS[:, :], in_=dnsc_h[:, :], inc=(dWG, 16))
    GE.dma_start(out=OSS[:, :], in_=osc_h[:, :], inc=(dWG, 16))
    GE.dma_start(out=MGC[:, :], in_=magic_h[:, :], inc=(dWG, 16))
    CONST_LD = dWG.v
    VE.memset(ZRO[:, :], 0)
    VE.memset(EPSC[:, :], EPS)
    VE.memset(NRM[:, 0, :], 0.0)
    VE.memset(NRM[:, 1, :], 0.0, inc=(vS, 1))

    TE.wait(dWG, CONST_LD)
    pwait(P[6])
    TE.matmul(P[6][:, 0:1], ROWS[:, 2, :], OSS[:, :], start=True, stop=True, inc=(tS, 1))
    VE.wait(tS)
    VE.tensor_copy(TD[:, 0:1], P[6][:, 0:1], inc=(vS, 1))
    pdone(P[6], vS)
    SE.wait(vS)
    SE.activation(CMB[:, :], DNS[:, :], AF.Copy, scale=TD[:, 0:1], inc=(cS, 1))
    CMB_DONE = cS.v

    def emit_ln(out_buf, osl, xready, outfree):
        VE.wait(*xready)
        SE.wait(*xready)
        lnmm = {}
        for dc in range(DC):
            b = dc % 2
            if dc >= 2:
                VE.wait(tS, lnmm[dc - 2])
                SE.wait(tS, lnmm[dc - 2])
            VE.tensor_copy(XBF[:, b, :], XB[:, dc, :], inc=(vS, 1))
            SE.activation(XSQ[:, b, :], XB[:, dc, :], AF.Square, inc=(cS, 1))
            TE.wait(vS)
            TE.wait(cS)
            if dc == 0:
                pwait(P[6])
                pwait(P[7])
            TE.matmul(P[6][0:1, :], ONEC[:, :], XBF[:, b, :], start=(dc == 0), stop=(dc == DC - 1))
            TE.matmul(P[7][0:1, :], ONEC[:, :], XSQ[:, b, :], start=(dc == 0), stop=(dc == DC - 1),
                      inc=(tS, 1))
            lnmm[dc] = tS.v
        VE.wait(tS)
        VE.tensor_scalar_mul(STAT[:, :, 0], P[6][0:1, :], 1.0 / D, inc=(vS, 1))
        VE.tensor_scalar_mul(STAT[:, :, 1], P[7][0:1, :], 1.0 / D, inc=(vS, 1))
        pdone(P[6], vS)
        pdone(P[7], vS)
        VE.wait(vS)
        VE.tensor_tensor(STAT[:, :, 2], STAT[:, :, 0], STAT[:, :, 0], ALU.mult, inc=(vS, 1))
        VE.wait(vS)
        VE.tensor_tensor(STAT[:, :, 1], STAT[:, :, 1], STAT[:, :, 2], ALU.subtract, inc=(vS, 1))
        SE.wait(vS)
        SE.activation(STAT[:, :, 1], STAT[:, :, 1], AF.Ln, bias=EPSC[0:1, :])
        SE.activation(STAT[:, :, 1], STAT[:, :, 1], AF.Exp, scale=-0.5, inc=(cS, 1))
        VE.wait(cS)
        VE.tensor_tensor(STAT[:, :, 2], STAT[:, :, 0], STAT[:, :, 1], ALU.mult, inc=(vS, 1))
        VE.wait(vS)
        TE.wait(vS)
        pwait(P[6])
        TE.matmul(P[6][:, :], ROWS[:, 0, :], STAT[:, :, 1], start=True, stop=True)
        pwait(P[7])
        TE.matmul(P[7][:, :], ROWS[:, 0, :], STAT[:, :, 2], start=True, stop=True, inc=(tS, 1))
        VE.wait(tS)
        VE.tensor_copy(RSB[:, :], P[6][:, :])
        VE.tensor_copy(MRSB[:, :], P[7][:, :], inc=(vS, 1))
        pdone(P[6], vS)
        pdone(P[7], vS)
        GE.wait(vS)
        if outfree is not None:
            GE.wait(*outfree)
        pol = {}
        for dc in range(DC):
            if dc > 1:
                VE.wait(pS, pol[dc - 2])
            VE.tensor_tensor(TF[:, dc % 2, :], XB[:, dc, :], RSB[:, :], ALU.mult, inc=(vS, 1))
            GE.wait(vS)
            GE.tensor_tensor(out_buf[:, dc, osl], TF[:, dc % 2, :], MRSB[:, :], ALU.subtract,
                             inc=(pS, 1))
            pol[dc] = pS.v
        return pS.v

    def hidsl(hc):
        if hc < 32:
            return KRB[:, hc // 2, (hc % 2) * TB:(hc % 2) * TB + TB]
        h2 = hc - 32
        return VTB[:, h2 // 4, (h2 % 4) * TB:(h2 % 4) * TB + TB]

    def emit_ffn(lnsl, ln_ready, wmode, hid_free, scr_gate=None):
        slab = {}
        amm = {}
        relu_done = {}
        sgn = {}
        scrw = {}
        NB = 4
        for step in range(HC + 1):
            if step < HC:
                hc = step
                b = hc % NB
                if wmode == "fp32":
                    if hc >= 1:
                        GE.wait(cS, sgn[hc - 1])
                    GE.dma_start(out=WF[:, :], in_=up_h[hc, :, :, :].rearrange("p a b -> p (a b)"), inc=(dWG, 16))
                    SE.wait(dWG)
                    SE.wait(*wfb_busy[b])
                    if hc >= NB:
                        SE.wait(dWG, scrw[hc - NB])
                    SE.activation(WFB[:, b, :], WF[:, :], AF.Sign, inc=(cS, 1))
                    sgn[hc] = cS.v
                    GE.wait(cS, sgn[hc])
                    GE.dma_start(out=upsgn_h[hc, :, :, :],
                                 in_=WFB[:, b, :].rearrange("p (a c) -> p a c", a=DC), inc=(dWG, 16))
                    scrw[hc] = dWG.v
                    slab[hc] = (cS, sgn[hc])
                else:
                    if hc == 0 and scr_gate is not None:
                        SY.wait(*scr_gate)
                    SY.wait(*wfb_busy[b])
                    SY.dma_start(out=WFB[:, b, :].rearrange("p (a c) -> p a c", a=DC),
                                 in_=upsgn_h[hc, :, :, :], inc=(dWS, 16))
                    slab[hc] = (dWS, dWS.v)
            if step >= 1:
                h = step - 1
                b = h % NB
                pb = P[h % 2]
                ssem, scnt = slab[h]
                TE.wait(ssem, scnt)
                if h == 0:
                    TE.wait(pS, ln_ready)
                    if hid_free is not None:
                        SE.wait(*hid_free)
                if h >= 2:
                    TE.wait(cS, relu_done[h - 2])
                else:
                    pwait(pb)
                for dc in range(DC):
                    TE.matmul(pb[:, :], WFB[:, b, dc * 128:dc * 128 + 128], LNB[:, dc, lnsl],
                              start=(dc == 0), stop=(dc == DC - 1),
                              inc=(tS, 1) if dc == DC - 1 else None)
                amm[h] = tS.v
                wfb_busy[b] = (tS, amm[h])
                SE.wait(tS, amm[h])
                SE.activation(hidsl(h), pb[:, :], AF.Relu, scale=UPS[:, h:h + 1], inc=(cS, 1))
                relu_done[h] = cS.v
                pdone(pb, cS)
        a_done = cS.v
        bmm = {}
        bsgn = {}
        bscrw = {}
        last_res = 0
        NB = 4
        for dcg in range(8):
            pwait(P[2])
            pwait(P[3])
            for hg8 in range(8):
                gi = dcg * 8 + hg8
                b = gi % NB
                if wmode == "fp32":
                    if gi >= 1:
                        GE.wait(cS, bsgn[gi - 1])
                    GE.dma_start(out=WF[:, :], in_=dn_h[dcg, :, hg8 * 8:hg8 * 8 + 8, :].rearrange("p a b -> p (a b)"),
                                 inc=(dWG, 16))
                    SE.wait(dWG)
                    SE.wait(*wfb_busy[b])
                    if gi >= NB:
                        SE.wait(dWG, bscrw[gi - NB])
                    SE.activation(WFB[:, b, :], WF[:, :], AF.Sign, inc=(cS, 1))
                    bsgn[gi] = cS.v
                    GE.wait(cS, bsgn[gi])
                    GE.dma_start(out=dnsgn_h[dcg, :, hg8 * 8:hg8 * 8 + 8, :],
                                 in_=WFB[:, b, :].rearrange("p (a c) -> p a c", a=8), inc=(dWG, 16))
                    bscrw[gi] = dWG.v
                    bload = (cS, bsgn[gi])
                else:
                    if gi == 0 and scr_gate is not None:
                        SY.wait(*scr_gate)
                    SY.wait(*wfb_busy[b])
                    SY.dma_start(out=WFB[:, b, :].rearrange("p (a c) -> p a c", a=8),
                                 in_=dnsgn_h[dcg, :, hg8 * 8:hg8 * 8 + 8, :], inc=(dWS, 16))
                    bload = (dWS, dWS.v)
                TE.wait(bload[0], bload[1])
                if gi == 0:
                    TE.wait(cS, a_done)
                for hloc in range(8):
                    hc = hg8 * 8 + hloc
                    st = (hg8 == 0 and hloc == 0)
                    sp = (hg8 == 7 and hloc == 7)
                    TE.matmul(P[2][:, :], WFB[:, b, hloc * 256:hloc * 256 + 128], hidsl(hc),
                              start=st, stop=sp)
                    TE.matmul(P[3][:, :], WFB[:, b, hloc * 256 + 128:hloc * 256 + 256], hidsl(hc),
                              start=st, stop=sp, inc=(tS, 1) if hloc == 7 else None)
                bmm[gi] = tS.v
                wfb_busy[b] = (tS, bmm[gi])
            SE.wait(tS, bmm[dcg * 8 + 7])
            SE.wait(cS, CMB_DONE)
            if dcg >= 1:
                SE.wait(vS, last_res)
            SE.activation(TD[:, :], P[2][:, :], AF.Copy, scale=CMB[:, 2 * dcg:2 * dcg + 1],
                          inc=(cS, 1))
            VE.wait(cS)
            VE.tensor_tensor(XB[:, 2 * dcg, :], XB[:, 2 * dcg, :], TD[:, :], ALU.add, inc=(vS, 1))
            SE.wait(vS)
            SE.activation(TD[:, :], P[3][:, :], AF.Copy, scale=CMB[:, 2 * dcg + 1:2 * dcg + 2],
                          inc=(cS, 1))
            pdone(P[2], cS)
            pdone(P[3], cS)
            VE.wait(cS)
            VE.tensor_tensor(XB[:, 2 * dcg + 1, :], XB[:, 2 * dcg + 1, :], TD[:, :], ALU.add,
                             inc=(vS, 1))
            last_res = vS.v
        return vS.v

    def emit_proj(out_buf, osl, lnsl, ln_ready, wsrc, rope, qoff, cs_ready, scr_gate=None):
        wload = {}
        pmm = {}
        emarks = {}
        ropemm = {}
        NB = 4
        for step in range(DC + 1):
            if step < DC:
                mc = step
                b = mc % NB
                if wsrc[0] == "cast":
                    GE.wait(*wfb_busy[b])
                    GE.dma_start(out=WFB[:, b, :].rearrange("p (a c) -> p a c", a=DC),
                                 in_=wsrc[1][mc, :, :, :], inc=(dWG, 16))
                    if wsrc[3]:
                        GE.wait(dWG)
                        GE.dma_start(out=wsrc[2][mc, :, :, :],
                                     in_=WFB[:, b, :].rearrange("p (a c) -> p a c", a=DC),
                                     inc=(dWG, 16))
                    wload[mc] = (dWG, dWG.v)
                else:
                    if mc == 0 and scr_gate is not None:
                        SY.wait(*scr_gate)
                    SY.wait(*wfb_busy[b])
                    SY.dma_start(out=WFB[:, b, :].rearrange("p (a c) -> p a c", a=DC),
                                 in_=wsrc[1][mc, :, :, :], inc=(dWS, 16))
                    wload[mc] = (dWS, dWS.v)
            if step >= 1:
                m = step - 1
                b = m % NB
                pb = P[m % 2]
                TE.wait(wload[m][0], wload[m][1])
                if m == 0:
                    TE.wait(pS, ln_ready)
                    if cs_ready is not None:
                        VE.wait(*cs_ready)
                        GE.wait(*cs_ready)
                if m >= 2:
                    TE.wait(cS, emarks[m - 2])
                else:
                    pwait(pb)
                for dc in range(DC):
                    TE.matmul(pb[:, :], WFB[:, b, dc * 128:dc * 128 + 128], LNB[:, dc, lnsl],
                              start=(dc == 0), stop=(dc == DC - 1),
                              inc=(tS, 1) if dc == DC - 1 else None)
                pmm[m] = tS.v
                wfb_busy[b] = (tS, pmm[m])
                if rope is None:
                    SE.wait(tS, pmm[m])
                    SE.activation(out_buf[:, m, osl], pb[:, :], AF.Copy, inc=(cS, 1))
                    emarks[m] = cS.v
                    pdone(pb, cS)
                else:
                    SE.wait(tS, pmm[m])
                    if m >= 1:
                        SE.wait(tS, ropemm[m - 1])
                    SE.activation(KPRE, pb[:, :], AF.Copy, inc=(cS, 1))
                    emarks[m] = cS.v
                    pdone(pb, cS)
                    VE.wait(cS, emarks[m])
                    GE.wait(cS, emarks[m])
                    GE.tensor_tensor(T1, KPRE, CSB[:, 1, qoff:qoff + TB], ALU.mult,
                                     inc=(pS, 1))
                    VE.tensor_tensor(T2, KPRE, CSB[:, 0, qoff:qoff + TB], ALU.mult,
                                     inc=(vS, 1))
                    TE.wait(pS)
                    TE.wait(vS)
                    pwait(P[4])
                    TE.matmul(P[4][:, :], PRM[:, :], T1, start=True, stop=False)
                    TE.matmul(P[4][:, :], IDB[:, :], T2, start=False, stop=True, inc=(tS, 1))
                    ropemm[m] = tS.v
                    SE.wait(tS, ropemm[m])
                    SE.activation(out_buf[:, m, osl], P[4][:, :], AF.Copy, inc=(cS, 1))
                    pdone(P[4], cS)
        return cS.v, tS.v

    def emit_vproj(lnsl, ln_ready, wsrc, tb, scr_gate=None):
        wload = {}
        pmm = {}
        evd = {}
        trd = {}
        NB = 4
        for step in range(DC + 1):
            if step < DC:
                mc = step
                b = mc % NB
                if wsrc[0] == "cast":
                    GE.wait(*wfb_busy[b])
                    GE.dma_start(out=WFB[:, b, :].rearrange("p (a c) -> p a c", a=DC),
                                 in_=wsrc[1][mc, :, :, :], inc=(dWG, 16))
                    if wsrc[3]:
                        GE.wait(dWG)
                        GE.dma_start(out=wsrc[2][mc, :, :, :],
                                     in_=WFB[:, b, :].rearrange("p (a c) -> p a c", a=DC),
                                     inc=(dWG, 16))
                    wload[mc] = (dWG, dWG.v)
                else:
                    if mc == 0 and scr_gate is not None:
                        SY.wait(*scr_gate)
                    SY.wait(*wfb_busy[b])
                    SY.dma_start(out=WFB[:, b, :].rearrange("p (a c) -> p a c", a=DC),
                                 in_=wsrc[1][mc, :, :, :], inc=(dWS, 16))
                    wload[mc] = (dWS, dWS.v)
            if step >= 1:
                m = step - 1
                b = m % NB
                pb = P[m % 2]
                TE.wait(wload[m][0], wload[m][1])
                if m == 0:
                    TE.wait(pS, ln_ready)
                if m >= 2:
                    TE.wait(vS, evd[m - 2])
                else:
                    pwait(pb)
                for dc in range(DC):
                    TE.matmul(pb[:, :], WFB[:, b, dc * 128:dc * 128 + 128], LNB[:, dc, lnsl],
                              start=(dc == 0), stop=(dc == DC - 1),
                              inc=(tS, 1) if dc == DC - 1 else None)
                pmm[m] = tS.v
                wfb_busy[b] = (tS, pmm[m])
                VE.wait(tS, pmm[m])
                if m >= 1:
                    VE.wait(tS, trd[m - 1])
                VE.tensor_copy(TF[:, m % 2, :], pb[:, :], inc=(vS, 1))
                evd[m] = vS.v
                pdone(pb, vS)
                TE.wait(vS, evd[m])
                for i in range(4):
                    pt = P[4 + (i % 2)]
                    pwait(pt)
                    TE.transpose(pt[:, 0:128], TF[:, m % 2, i * 128:i * 128 + 128], IDN[:, :],
                                 inc=(tS, 1))
                    SE.wait(tS)
                    SE.activation(VTB[:, tb * 4 + i, m * 128:m * 128 + 128], pt[:, 0:128],
                                  AF.Copy, inc=(cS, 1))
                    pdone(pt, cS)
                trd[m] = tS.v
        return cS.v

    def emit_norms(buf, ncols, col, ready):
        SE.wait(*ready)
        for h in range(NH):
            SE.activation(KSQ[:, 0:ncols], buf[:, h, 0:ncols], AF.Square, inc=(cS, 1))
            TE.wait(cS)
            pwait(P[6])
            TE.matmul(P[6][0:1, :], ONEC[:, :], KSQ[:, 0:TB], start=True, stop=True)
            pwait(P[7])
            TE.matmul(P[7][0:1, :], ONEC[:, :], KSQ[:, TB:2 * TB], start=True, stop=True,
                      inc=(tS, 1))
            VE.wait(tS)
            VE.reduce_max(KNT[:, 0:1], P[6][0:1, :], axis=AX.X, inc=(vS, 1))
            VE.reduce_max(KNT[:, 1:2], P[7][0:1, :], axis=AX.X, inc=(vS, 1))
            pdone(P[6], vS)
            pdone(P[7], vS)
            VE.wait(vS)
            VE.tensor_tensor(KNT[:, 0:1], KNT[:, 0:1], KNT[:, 1:2], ALU.max, inc=(vS, 1))
            VE.wait(vS)
            VE.tensor_tensor(NRM[:, col, h:h + 1], NRM[:, col, h:h + 1], KNT[:, 0:1], ALU.max,
                             inc=(vS, 1))
        return vS.v

    def arm_info(arm):
        log = physical_to_logical_core_id(int(arm) % 128, arm.node_id_4,
                                          container_num_cores=8) % 8
        return log, log % 2

    regs = {}

    xb_free_v = 0
    xb_free_io = 0
    lnb_free = {0: (tS, 0), 1: (tS, 0)}
    hid_free = None
    ffn_scr = None
    k_scr = None
    v_scr = None
    TBSL = [slice(0, TB), slice(TB, 2 * TB)]

    for slot in range(COV):
        SY.wait(pS)
        SY.wait(vS)
        SY.dma_start(out=CSB[:, 0, :], in_=cscov_h[slot, 0, :, :], inc=(dIO, 16))
        SY.dma_start(out=CSB[:, 1, :], in_=cscov_h[slot, 1, :, :], inc=(dIO, 16))
        cs_ready = (dIO, dIO.v)
        for tb in range(2):
            wmode = "fp32" if (slot == 0 and tb == 0) else "bf16"
            SY.wait(vS, xb_free_v)
            SY.wait(dIO, xb_free_io)
            SY.dma_start(out=XB[:, :, :], in_=xcov_h[slot, :, :, TBSL[tb]], inc=(dIO, 16))
            xready = (dIO, dIO.v)
            lnr = emit_ln(LNB, TBSL[tb], xready, lnb_free[tb])
            res = emit_ffn(TBSL[tb], lnr, wmode, hid_free, scr_gate=ffn_scr)
            if slot == 0 and tb == 0:
                ffn_scr = (dWG, dWG.v)
            hid_free = None
            lnb_free[tb] = (tS, tS.v)
            SY.wait(vS, res)
            SY.dma_start(out=xmid_h[slot, :, :, TBSL[tb]], in_=XB[:, :, :], inc=(dIO, 16))
            xb_free_v = res
            xb_free_io = dIO.v
        for tb in range(2):
            SY.wait(vS, xb_free_v)
            SY.wait(dIO, xb_free_io)
            SY.dma_start(out=XB[:, :, :], in_=xmid_h[slot, :, :, TBSL[tb]], inc=(dIO, 16))
            xready = (dIO, dIO.v)
            lnr = emit_ln(LNB, TBSL[tb], xready, lnb_free[tb])
            ksrc = ("cast", wk_h, wkbf_h, True) if (slot == 0 and tb == 0) else ("scr", wkbf_h)
            kdone, kmm = emit_proj(KRB, TBSL[tb], TBSL[tb], lnr, ksrc, "rope", tb * TB,
                                   cs_ready, scr_gate=k_scr)
            if slot == 0 and tb == 0:
                k_scr = (dWG, dWG.v)
            vsrc = ("cast", wv_h, wvbf_h, True) if (slot == 0 and tb == 0) else ("scr", wvbf_h)
            vdone = emit_vproj(TBSL[tb], lnr, vsrc, tb, scr_gate=v_scr)
            if slot == 0 and tb == 0:
                v_scr = (dWG, dWG.v)
            lnb_free[tb] = (tS, tS.v)
            xb_free_v = vS.v
        emit_norms(KRB, TPC, 0, (cS, cS.v))
        SY.wait(cS)
        SY.wait(vS)
        kvw = []
        for hq in range(4):
            dSH.v += 16
        for tt in range(8):
            dSH.v += 16
        pre = dSH.v

        def mk_kv_writer(slotc, target):
            def go(sy):
                for arm in nc.switch_core_id([nc.sync], n=128):
                    log, par = arm_info(arm)
                    g = 2 * slotc + par
                    for hq in range(4):
                        sy.dma_start(
                            out=shk_h[:, hq * 4:hq * 4 + 4, g * 1024:(g + 1) * 1024],
                            in_=KRB[:, hq * 4:hq * 4 + 4, :],
                        ).then_inc(target, 16)
                    for tt in range(8):
                        sy.dma_start(
                            out=shv_h[g * 1024 + tt * 128:g * 1024 + (tt + 1) * 128, :],
                            in_=VTB[:, tt, :],
                        ).then_inc(target, 16)
            return go

        SY.raw(mk_kv_writer(slot, dSH.h))
        hid_free = (dSH, dSH.v)

    knorm_ready = vS.v
    SY.wait(vS, knorm_ready)
    all_sh = dSH.v
    dSH.v += 16
    dSH.v += 16
    dSH.v += 16
    BAR_END = dSH.v

    def barrier(sy):
        r_mg = sy.register("r_mg").__enter__()
        r_pf = sy.register("r_pf").__enter__()
        r_nd = sy.register("r_nd").__enter__()
        regs["mg"], regs["pf"], regs["nd"] = r_mg, r_pf, r_nd
        for arm in nc.switch_core_id([nc.sync], n=128):
            log, par = arm_info(arm)
            other = 1 - par
            sy.dma_start(out=shkn_h[par:par + 1, :], in_=NRM[:, 0, :]).then_inc(dSH.h, 16)
            sy.wait_ge(dSH.h, all_sh + 16)
            sy.dma_start(out=shfl_h[par:par + 1, :], in_=MGC[:, :]).then_inc(dSH.h, 16)
            sy.wait_ge(dSH.h, all_sh + 32)
            sy.load(r_mg, MGC[0:1, 0:1])
            sy.reg_mov(r_nd, 1)
            with sy.While(r_nd):
                sy.load(r_pf, shfl_h[other:other + 1, 0:1])
                sy.reg_alu(r_nd, r_pf, r_mg, ALU.not_equal)
            sy.dma_start(out=PKN[:, :], in_=shkn_h[other:other + 1, :]).then_inc(dSH.h, 16)

    SY.raw(barrier)
    pkn_ready = BAR_END

    SY.wait(pS)
    SY.wait(vS)
    SY.dma_start(out=CSB[:, 0, :], in_=csown_h[0, :, :], inc=(dIO, 16))
    SY.dma_start(out=CSB[:, 1, :], in_=csown_h[1, :, :], inc=(dIO, 16))
    cs_own = (dIO, dIO.v)

    qdone = 0
    for tb in range(2):
        SY.wait(vS, xb_free_v)
        SY.wait(dIO, xb_free_io)
        dIO.v += 16

        def mk_xmid_rd(tbc, target, cnt):
            def go(sy):
                for arm in nc.switch_core_id([nc.sync], n=128):
                    log, par = arm_info(arm)
                    myslot = log // 2
                    sy.dma_start(out=XB[:, :, :],
                                 in_=xmid_h[myslot, :, :, TBSL[tbc]]).then_inc(target, 16)
                sy.wait_ge(target, cnt)
            return go

        SY.raw(mk_xmid_rd(tb, dIO.h, dIO.v))
        xready = (dIO, dIO.v)
        lnr = emit_ln(LNB, TBSL[tb], xready, lnb_free[tb])
        qsrc = ("cast", wq_h, None, False)
        qdone, _ = emit_proj(KRB, TBSL[tb], TBSL[tb], lnr, qsrc, "rope", tb * TB, cs_own)
        lnb_free[tb] = (tS, tS.v)
        xb_free_v = vS.v
    emit_norms(KRB, TPC, 1, (cS, qdone))

    VE.wait(dSH, pkn_ready)
    VE.wait(vS)
    VE.tensor_tensor(NRM[:, 2, :], NRM[:, 0, :], PKN[:, :], ALU.max, inc=(vS, 1))
    VE.wait(vS)
    VE.tensor_tensor(NRM[:, 2, :], NRM[:, 2, :], NRM[:, 1, :], ALU.mult, inc=(vS, 1))
    VE.wait(vS)
    VE.tensor_scalar_max(NRM[:, 2, :], NRM[:, 2, :], 1e-6, inc=(vS, 1))
    SE.wait(vS)
    SE.activation(NRM[:, 3, :], NRM[:, 2, :], AF.Sqrt, scale=1.0 / 128.0, inc=(cS, 1))
    VE.wait(cS)
    VE.tensor_scalar(NRM[:, 3, :], NRM[:, 3, :], 1.01, 0.5, ALU.mult, op1=ALU.add, inc=(vS, 1))
    TE.wait(vS)
    pwait(P[6])
    TE.matmul(P[6][:, 0:16], ROWS[:, 1, :], NRM[:, 3, :], start=True, stop=True, inc=(tS, 1))
    VE.wait(tS)
    VE.tensor_copy(NEGM[:, :], P[6][:, 0:16], inc=(vS, 1))
    pdone(P[6], vS)
    negm_ready = vS.v

    q_ready = qdone

    exp_done = {}
    pv_mm = {}
    acc_done = {}
    scr_free = {}
    atn_marks = []
    for qh in range(2):
        qsl = slice(qh * TB, qh * TB + TB)
        for (h0, h1) in HGROUPS:
            g = h1 - h0
            kready = {}
            vready = {}
            kcons = {}
            pvdone = {}

            def kload(j):
                if j >= 2:
                    SY.wait(tS, kcons[j - 2])
                for hi in range(g):
                    SY.dma_start(out=KSL[:, j % 2, hi, :],
                                 in_=shk_h[:, h0 + hi, j * 256:(j + 1) * 256],
                                 inc=(dKV, 16))
                kready[j] = dKV.v

            def vload(kc):
                if kc >= 2:
                    SY.wait(tS, pvdone[kc - 2])
                SY.dma_start(out=VSL[:, kc % 2, 0:g * 128],
                             in_=shv_h[kc * 128:(kc + 1) * 128, h0 * 128:h1 * 128],
                             inc=(dKV, 16))
                vready[kc] = dKV.v

            SY.wait(dSH, pkn_ready)
            kload(0)
            kload(1)
            vload(0)
            vload(1)
            it = 0
            for j in range(32):
                TE.wait(dKV, kready[j])
                for tt in range(2):
                    kc = j * 2 + tt
                    TE.wait(dKV, vready[kc])
                    for hi in range(g):
                        h = h0 + hi
                        sbk = P[6] if it % 2 == 0 else P[7]
                        if it < 2:
                            pwait(sbk)
                        else:
                            TE.wait(cS, exp_done[it - 2])
                        if it == 0:
                            TE.wait(cS, q_ready)
                            TE.wait(vS, negm_ready)
                        TE.matmul(sbk[:, :], KSL[:, j % 2, hi, tt * 128:(tt + 1) * 128],
                                  KRB[:, h, qsl], start=True, stop=True, inc=(tS, 1))
                        smm = tS.v
                        SE.wait(tS, smm)
                        if it >= 4:
                            SE.wait(vS, acc_done[it - 4])
                        SE.activation(EXB[:, it % 4, :], sbk[:, :], AF.Exp,
                                      bias=NEGM[:, h:h + 1], scale=INVSQHD, inc=(cS, 1))
                        exp_done[it] = cS.v
                        VE.wait(cS, exp_done[it])
                        if kc == 0:
                            VE.tensor_copy(ACC[:, hi, :], EXB[:, it % 4, :], inc=(vS, 1))
                        else:
                            VE.tensor_tensor(ACC[:, hi, :], ACC[:, hi, :], EXB[:, it % 4, :],
                                             ALU.add, inc=(vS, 1))
                        acc_done[it] = vS.v
                        TE.wait(cS, exp_done[it])
                        if kc == 0:
                            pwait(P[hi])
                        TE.matmul(P[hi][:, :], VSL[:, kc % 2, hi * 128:(hi + 1) * 128],
                                  EXB[:, it % 4, :], start=(kc == 0), stop=(kc == 63),
                                  inc=(tS, 1))
                        pv_mm[it] = tS.v
                        it += 1
                    pvdone[kc] = tS.v
                    if kc + 2 < 64:
                        vload(kc + 2)
                kcons[j] = tS.v
                if j + 2 < 32:
                    kload(j + 2)
            last_pv = tS.v
            for hi in range(g):
                h = h0 + hi
                VE.wait(tS, last_pv)
                VE.tensor_copy(T1, ACC[:, hi, :], inc=(vS, 1))
                TE.wait(vS)
                pwait(P[6])
                TE.matmul(P[6][0:1, :], ONEC[:, :], T1, start=True, stop=True, inc=(tS, 1))
                VE.wait(tS)
                VE.reciprocal(STAT[:, :, 0], P[6][0:1, :], inc=(vS, 1))
                pdone(P[6], vS)
                TE.wait(vS)
                pwait(P[7])
                TE.matmul(P[7][:, :], ROWS[:, 0, :], STAT[:, :, 0], start=True, stop=True,
                          inc=(tS, 1))
                VE.wait(tS)
                VE.tensor_copy(RSB[:, :], P[7][:, :], inc=(vS, 1))
                pdone(P[7], vS)
                VE.tensor_tensor(ATN[:, h * 1024 + qh * TB:h * 1024 + qh * TB + TB],
                                 P[hi][:, :], RSB[:, :], ALU.mult, inc=(vS, 1))
                pdone(P[hi], vS)
            atn_marks.append(vS.v)
    atn_ready = vS.v

    for tb in range(2):
        SY.wait(vS, xb_free_v)
        SY.wait(dIO, xb_free_io)
        dIO.v += 16
        SY.raw(mk_xmid_rd(tb, dIO.h, dIO.v))
        xready = (dIO, dIO.v)
        owload = {}
        omm = {}
        oev = {}
        NB = 4
        for step in range(DC + 1):
            if step < DC:
                mc = step
                b = mc % NB
                GE.wait(*wfb_busy[b])
                GE.dma_start(out=WFB[:, b, :].rearrange("p (a c) -> p a c", a=DC),
                             in_=wo_h[mc, :, :, :], inc=(dWG, 16))
                owload[mc] = (dWG, dWG.v)
            if step >= 1:
                m = step - 1
                b = m % NB
                pb = P[m % 2]
                TE.wait(owload[m][0], owload[m][1])
                if m == 0:
                    TE.wait(vS, atn_ready)
                if m >= 2:
                    TE.wait(vS, oev[m - 2])
                else:
                    pwait(pb)
                for dc in range(DC):
                    TE.matmul(pb[:, :], WFB[:, b, dc * 128:dc * 128 + 128],
                              ATN[:, dc * 1024 + tb * TB:dc * 1024 + tb * TB + TB],
                              start=(dc == 0), stop=(dc == DC - 1),
                              inc=(tS, 1) if dc == DC - 1 else None)
                omm[m] = tS.v
                wfb_busy[b] = (tS, omm[m])
                VE.wait(tS, omm[m])
                if m == 0:
                    VE.wait(*xready)
                VE.tensor_tensor(XB[:, m, :], XB[:, m, :], pb[:, :], ALU.add, inc=(vS, 1))
                oev[m] = vS.v
                pdone(pb, vS)
        SY.wait(vS, oev[DC - 1])
        SY.dma_start(out=x1_h[:, :, TBSL[tb]], in_=XB[:, :, :], inc=(dIO, 16))
        xb_free_v = vS.v
        xb_free_io = dIO.v
    o_done = tS.v

    hid_free = (tS, o_done)
    for tb in range(2):
        SY.wait(vS, xb_free_v)
        SY.wait(dIO, xb_free_io)
        SY.dma_start(out=XB[:, :, :], in_=x1_h[:, :, TBSL[tb]], inc=(dIO, 16))
        xready = (dIO, dIO.v)
        lnr = emit_ln(LNB, TBSL[tb], xready, lnb_free[tb])
        res = emit_ffn(TBSL[tb], lnr, "bf16", hid_free, scr_gate=ffn_scr)
        hid_free = None
        lnb_free[tb] = (tS, tS.v)
        SY.wait(vS, res)
        SY.dma_start(out=out_h[:, :, TBSL[tb]], in_=XB[:, :, :], inc=(dIO, 16))
        xb_free_v = res
        xb_free_io = dIO.v

    dIO.v += 16

    def resetter(sy):
        for arm in nc.switch_core_id([nc.sync], n=128):
            log, par = arm_info(arm)
            sy.dma_start(out=shfl_h[par:par + 1, :], in_=ZRO[:, :]).then_inc(dIO.h, 16)

    SY.raw(resetter)
    SY.wait(dIO)
    SY.wait(dSH)
    SY.wait(dKV)

    with nc.Block() as block:
        def replay(engine_name):
            def body(eng):
                for (en, op, a, k, inc) in script:
                    if en != engine_name:
                        continue
                    if op is None:
                        a[0](eng)
                        continue
                    inst = getattr(eng, op)(*a, **k)
                    if inc is not None:
                        inst.then_inc(inc[0], inc[1])
            return body

        block.tensor(replay("tensor"))
        block.vector(replay("vector"))
        block.scalar(replay("scalar"))
        block.gpsimd(replay("gpsimd"))
        block.sync(replay("sync"))

    for t in reversed(ctx):
        t.__exit__(None, None, None)
    return nc


_CACHE = {}


def _stage_weights(L0_attn_wq, L0_attn_wk, L0_attn_wv, L0_attn_wo,
                   L0_ffn_up_weight, L0_ffn_down_weight,
                   L0_ffn_up_scale, L0_ffn_down_scale, L0_ffn_output_scale):
    st = {}

    def wstage(w):
        return np.ascontiguousarray(w.reshape(DC, 128, DC, 128).transpose(0, 3, 2, 1))

    st["wqt"] = wstage(L0_attn_wq)
    st["wkt"] = wstage(L0_attn_wk)
    st["wvt"] = wstage(L0_attn_wv)
    st["wot"] = wstage(L0_attn_wo)
    st["upt"] = np.ascontiguousarray(
        L0_ffn_up_weight.reshape(HC, 128, DC, 128).transpose(0, 3, 2, 1))
    dw = L0_ffn_down_weight.reshape(8, 2, 128, HC, 128)
    st["dnt"] = np.ascontiguousarray(
        dw.transpose(0, 4, 3, 1, 2).reshape(8, 128, HC, 256))
    st["upsc"] = np.ascontiguousarray(L0_ffn_up_scale.reshape(HC, 128).T)
    st["dnsc"] = np.ascontiguousarray(L0_ffn_down_scale.reshape(DC, 128).T)
    st["osc"] = np.ascontiguousarray(L0_ffn_output_scale.reshape(1, 1))
    return st


def _consts():
    cf = np.zeros((128, 648), np.float32)
    prm = np.zeros((128, 128), np.float32)
    for i in range(64):
        prm[2 * i + 1, 2 * i] = 1.0
        prm[2 * i, 2 * i + 1] = 1.0
    cf[:, 0:128] = prm
    cf[:, 128:256] = np.eye(128, dtype=np.float32)
    cf[:, 256] = 1.0
    cf[0, 257:385] = 1.0
    cf[0, 385:513] = -1.0
    cf[0, 513:641] = 0.5
    return cf


def _cs_tables():
    import ml_dtypes
    hd = 128
    scale = np.arange(0, hd, 2, dtype=np.float64) / hd
    inv = 1.0 / 10000.0 ** scale
    t = np.arange(SEQ, dtype=np.float64)
    ang = np.outer(inv, t)
    C = np.empty((128, SEQ), np.float64)
    S2 = np.empty((128, SEQ), np.float64)
    C[0::2] = np.cos(ang)
    C[1::2] = np.cos(ang)
    S2[0::2] = np.sin(ang)
    S2[1::2] = -np.sin(ang)
    return (C.astype(ml_dtypes.bfloat16), S2.astype(ml_dtypes.bfloat16))


def kernel(L0_attn_x, L0_attn_wq, L0_attn_wk, L0_attn_wv, L0_attn_wo,
           L0_ffn_up_weight, L0_ffn_down_weight, L0_ffn_up_scale,
           L0_ffn_down_scale, L0_ffn_output_scale):
    import ml_dtypes
    if "nc" not in _CACHE:
        _CACHE["nc"] = build_program()
    nc = _CACHE["nc"]

    st = _stage_weights(np.asarray(L0_attn_wq, np.float32), np.asarray(L0_attn_wk, np.float32),
                        np.asarray(L0_attn_wv, np.float32), np.asarray(L0_attn_wo, np.float32),
                        np.asarray(L0_ffn_up_weight, np.float32),
                        np.asarray(L0_ffn_down_weight, np.float32),
                        np.asarray(L0_ffn_up_scale, np.float32),
                        np.asarray(L0_ffn_down_scale, np.float32),
                        np.asarray(L0_ffn_output_scale, np.float32))
    cf = _consts()
    C, S2 = _cs_tables()
    x = np.asarray(L0_attn_x, np.float32).reshape(SEQ, D)
    magic = np.random.randint(1, 2 ** 31 - 1, size=(1, NMAGIC)).astype(np.int32)

    def fm(chunk):
        blk = x[chunk * 1024:(chunk + 1) * 1024, :]
        return np.ascontiguousarray(blk.T.reshape(DC, 128, 1024).transpose(1, 0, 2))

    in_maps = []
    for c in range(8):
        par = c % 2
        chunks = [par, par + 2, par + 4, par + 6]
        xcov = np.stack([fm(g) for g in chunks], axis=0)
        cscov = np.stack(
            [np.stack([C[:, g * 1024:(g + 1) * 1024], S2[:, g * 1024:(g + 1) * 1024]], axis=0)
             for g in chunks], axis=0)
        csown = np.stack([C[:, c * 1024:(c + 1) * 1024], S2[:, c * 1024:(c + 1) * 1024]], axis=0)
        m = dict(st)
        m["xcov"] = xcov
        m["cscov"] = np.ascontiguousarray(cscov)
        m["csown"] = np.ascontiguousarray(csown)
        m["constf"] = cf
        m["magic"] = magic
        in_maps.append(m)

    res = run_bass_kernel_spmd(nc, in_maps, core_ids=list(range(8)))
    out = np.empty((SEQ, D), np.float32)
    for c in range(8):
        arr = res.results[c]["out"]
        out[c * 1024:(c + 1) * 1024, :] = arr.transpose(1, 0, 2).reshape(D, 1024).T
    return out.reshape(1, SEQ, D)


# version 05
# speedup vs baseline: 1.1145x; 1.1145x over previous
import numpy as np

import concourse.bass as bass
import concourse.mybir as mybir
from concourse.bass_utils import run_bass_kernel_spmd
from concourse.libnrt import physical_to_logical_core_id

F32 = mybir.dt.float32
BF16 = mybir.dt.bfloat16
I32 = mybir.dt.int32
AF = mybir.ActivationFunctionType
ALU = mybir.AluOpType
AX = mybir.AxisListType

D = 2048
HID = 8192
SEQ = 8192
NH = 16
TPC = 1024
COV = 4
TB = 512
DC = 16
HC = 64
NMAGIC = 4
INVSQHD = float(1.0 / np.sqrt(128.0))
EPS = 1e-5
HGROUPS = [(0, 6), (6, 12), (12, 16)]


class Sem:
    def __init__(self, handle):
        self.h = handle
        self.v = 0


class Rec:

    def __init__(self, script, name):
        self.__dict__["_s"] = script
        self.__dict__["_n"] = name
        self.__dict__["_seen"] = {}

    def __getattr__(self, op):
        def rec(*a, inc=None, **k):
            pair = None
            if inc is not None:
                sem, n = inc
                sem.v += n
                pair = (sem.h, n)
            self._s.append((self._n, op, a, k, pair))
        return rec

    def wait(self, sem, v=None):
        t = sem.v if v is None else v
        if t <= 0:
            return
        if self._seen.get(id(sem), -1) >= t:
            return
        self._seen[id(sem)] = t
        self._s.append((self._n, "wait_ge", (sem.h, t), {}, None))

    def raw(self, fn):
        self._s.append((self._n, None, (fn,), {}, None))


def build_program():
    nc = bass.Bass("TRN2", target_bir_lowering=False)

    xcov_h = nc.dram_tensor("xcov", [COV, 128, DC, TPC], F32, kind="ExternalInput").ap()
    wq_h = nc.dram_tensor("wqt", [DC, 128, DC, 128], F32, kind="ExternalInput").ap()
    wk_h = nc.dram_tensor("wkt", [DC, 128, DC, 128], F32, kind="ExternalInput").ap()
    wv_h = nc.dram_tensor("wvt", [DC, 128, DC, 128], F32, kind="ExternalInput").ap()
    wo_h = nc.dram_tensor("wot", [DC, 128, DC, 128], F32, kind="ExternalInput").ap()
    up_h = nc.dram_tensor("upt", [HC, 128, DC, 128], F32, kind="ExternalInput").ap()
    dn_h = nc.dram_tensor("dnt", [8, 128, HC, 256], F32, kind="ExternalInput").ap()
    upsc_h = nc.dram_tensor("upsc", [128, HC], F32, kind="ExternalInput").ap()
    dnsc_h = nc.dram_tensor("dnsc", [128, DC], F32, kind="ExternalInput").ap()
    osc_h = nc.dram_tensor("osc", [1, 1], F32, kind="ExternalInput").ap()
    cscov_h = nc.dram_tensor("cscov", [COV, 2, 128, TPC], BF16, kind="ExternalInput").ap()
    csown_h = nc.dram_tensor("csown", [2, 128, TPC], BF16, kind="ExternalInput").ap()
    cf_h = nc.dram_tensor("constf", [128, 648], F32, kind="ExternalInput").ap()
    magic_h = nc.dram_tensor("magic", [1, NMAGIC], I32, kind="ExternalInput").ap()
    out_h = nc.dram_tensor("out", [128, DC, TPC], F32, kind="ExternalOutput").ap()

    upsgn_h = nc.dram_tensor("upsgn", [HC, 128, DC, 128], BF16, kind="Internal").ap()
    dnsgn_h = nc.dram_tensor("dnsgn", [8, 128, HC, 256], BF16, kind="Internal").ap()
    wkbf_h = nc.dram_tensor("wkbf", [DC, 128, DC, 128], BF16, kind="Internal").ap()
    wvbf_h = nc.dram_tensor("wvbf", [DC, 128, DC, 128], BF16, kind="Internal").ap()
    xmid_h = nc.dram_tensor("xmidscr", [COV, 128, DC, TPC], F32, kind="Internal").ap()
    x1_h = nc.dram_tensor("x1scr", [128, DC, TPC], F32, kind="Internal").ap()

    shk_h = nc.dram_tensor("shk", [128, NH, SEQ], BF16, kind="Internal", addr_space="Shared").ap()
    shv_h = nc.dram_tensor("shv", [SEQ, D], BF16, kind="Internal", addr_space="Shared").ap()
    shkn_h = nc.dram_tensor("shkn", [2, 16], F32, kind="Internal", addr_space="Shared").ap()
    shfl_h = nc.dram_tensor("shfl", [2, NMAGIC], I32, kind="Internal", addr_space="Shared").ap()

    ctx = []

    def enter(t):
        ctx.append(t)
        return t.__enter__()

    sbuf = lambda n, sh, dt: enter(nc.sbuf_tensor(n, sh, dt))
    psum = lambda n: enter(nc.psum_tensor(n, [128, 512], F32))

    XB = sbuf("XB", [128, DC, TB], F32)
    LNB = sbuf("LNB", [128, DC, TPC], BF16)
    KRB = sbuf("KRB", [128, NH, TPC], BF16)
    VTB = sbuf("VTB", [128, 8, D], BF16)
    WF = sbuf("WF", [128, 2048], F32)
    WFB = sbuf("WFB", [128, 4, 2048], BF16)
    KSL = sbuf("KSL", [128, 2, 6, 256], BF16)
    VSL = sbuf("VSL", [128, 2, 768], BF16)
    ACC = sbuf("ACC", [128, 6, TB], F32)
    EXB = sbuf("EXB", [128, 4, TB], BF16)
    KPRE = EXB[:, 2, :]
    KSQ = EXB[:, 0:2, :].rearrange("p a b -> p (a b)")
    T1 = EXB[:, 0, :]
    T2 = EXB[:, 1, :]
    TF = sbuf("TF", [128, 2, TB], F32)
    TD = sbuf("TD", [128, TB], F32)
    RSB = sbuf("RSB", [128, TB], F32)
    MRSB = sbuf("MRSB", [128, TB], F32)
    CSB = sbuf("CSB", [128, 2, TPC], BF16)
    XBF = sbuf("XBF", [128, 2, TB], BF16)
    XSQ = sbuf("XSQ", [128, 2, TB], BF16)
    STAT = sbuf("STAT", [1, TB, 3], F32)
    PRM = sbuf("PRM", [128, 128], BF16)
    IDB = sbuf("IDB", [128, 128], BF16)
    IDN = sbuf("IDN", [128, 128], F32)
    ONEC = sbuf("ONEC", [128, 1], BF16)
    ROWS = sbuf("ROWS", [1, 3, 128], F32)
    UPS = sbuf("UPS", [128, HC], F32)
    DNS = sbuf("DNS", [128, DC], F32)
    CMB = sbuf("CMB", [128, DC], F32)
    OSS = sbuf("OSS", [1, 1], F32)
    NRM = sbuf("NRM", [1, 4, 16], F32)
    KNT = sbuf("KNT", [1, 2], F32)
    NEGM = sbuf("NEGM", [128, 16], F32)
    EPSC = sbuf("EPSC", [128, 1], F32)
    MGC = sbuf("MGC", [1, NMAGIC], I32)
    ZRO = sbuf("ZRO", [1, NMAGIC], I32)
    PKN = sbuf("PKN", [1, 16], F32)

    P = [psum(f"pp{i}") for i in range(8)]
    ATN = VTB[:, :, :].rearrange("p a b -> p (a b)")

    sems = {}

    def mksem(n):
        s = Sem(enter(nc.semaphore(n)))
        sems[n] = s
        return s

    dIO = mksem("dIO")
    dSH = mksem("dSH")
    dKV = mksem("dKV")
    dWS = mksem("dWS")
    dWG = mksem("dWG")
    tS = mksem("tS")
    vS = mksem("vS")
    cS = mksem("cS")
    pS = mksem("pS")

    script = []
    TE = Rec(script, "tensor")
    VE = Rec(script, "vector")
    SE = Rec(script, "scalar")
    GE = Rec(script, "gpsimd")
    SY = Rec(script, "sync")

    bguard = {}
    wfb_busy = {0: (tS, 0), 1: (tS, 0), 2: (tS, 0), 3: (tS, 0)}

    def pwait(p):
        g = bguard.pop(id(p), None)
        if g is not None:
            TE.wait(g[0], g[1])

    def pdone(p, sem):
        bguard[id(p)] = (sem, sem.v)

    GE.dma_start(out=PRM[:, :], in_=cf_h[:, 0:128], inc=(dWG, 16))
    GE.dma_start(out=IDB[:, :], in_=cf_h[:, 128:256], inc=(dWG, 16))
    GE.dma_start(out=IDN[:, :], in_=cf_h[:, 128:256], inc=(dWG, 16))
    GE.dma_start(out=ONEC[:, :], in_=cf_h[:, 256:257], allow_slow_non_contiguous=True, inc=(dWG, 16))
    GE.dma_start(out=ROWS[:, 0, :], in_=cf_h[0:1, 257:385], inc=(dWG, 16))
    GE.dma_start(out=ROWS[:, 1, :], in_=cf_h[0:1, 385:513], inc=(dWG, 16))
    GE.dma_start(out=ROWS[:, 2, :], in_=cf_h[0:1, 513:641], inc=(dWG, 16))
    GE.dma_start(out=UPS[:, :], in_=upsc_h[:, :], inc=(dWG, 16))
    GE.dma_start(out=DNS[:, :], in_=dnsc_h[:, :], inc=(dWG, 16))
    GE.dma_start(out=OSS[:, :], in_=osc_h[:, :], inc=(dWG, 16))
    GE.dma_start(out=MGC[:, :], in_=magic_h[:, :], inc=(dWG, 16))
    CONST_LD = dWG.v
    VE.memset(ZRO[:, :], 0)
    VE.memset(EPSC[:, :], EPS)
    VE.memset(NRM[:, 0, :], 0.0)
    VE.memset(NRM[:, 1, :], 0.0, inc=(vS, 1))

    TE.wait(dWG, CONST_LD)
    pwait(P[6])
    TE.matmul(P[6][:, 0:1], ROWS[:, 2, :], OSS[:, :], start=True, stop=True, inc=(tS, 1))
    VE.wait(tS)
    VE.tensor_copy(TD[:, 0:1], P[6][:, 0:1], inc=(vS, 1))
    pdone(P[6], vS)
    SE.wait(vS)
    SE.activation(CMB[:, :], DNS[:, :], AF.Copy, scale=TD[:, 0:1], inc=(cS, 1))
    CMB_DONE = cS.v

    def emit_ln(out_buf, osl, xready, outfree):
        VE.wait(*xready)
        SE.wait(*xready)
        lnmm = {}
        for dc in range(DC):
            b = dc % 2
            if dc >= 2:
                VE.wait(tS, lnmm[dc - 2])
                SE.wait(tS, lnmm[dc - 2])
            VE.tensor_copy(XBF[:, b, :], XB[:, dc, :], inc=(vS, 1))
            SE.activation(XSQ[:, b, :], XB[:, dc, :], AF.Square, inc=(cS, 1))
            TE.wait(vS)
            TE.wait(cS)
            if dc == 0:
                pwait(P[6])
                pwait(P[7])
            TE.matmul(P[6][0:1, :], ONEC[:, :], XBF[:, b, :], start=(dc == 0), stop=(dc == DC - 1))
            TE.matmul(P[7][0:1, :], ONEC[:, :], XSQ[:, b, :], start=(dc == 0), stop=(dc == DC - 1),
                      inc=(tS, 1))
            lnmm[dc] = tS.v
        VE.wait(tS)
        VE.tensor_scalar_mul(STAT[:, :, 0], P[6][0:1, :], 1.0 / D, inc=(vS, 1))
        VE.tensor_scalar_mul(STAT[:, :, 1], P[7][0:1, :], 1.0 / D, inc=(vS, 1))
        pdone(P[6], vS)
        pdone(P[7], vS)
        VE.wait(vS)
        VE.tensor_tensor(STAT[:, :, 2], STAT[:, :, 0], STAT[:, :, 0], ALU.mult, inc=(vS, 1))
        VE.wait(vS)
        VE.tensor_tensor(STAT[:, :, 1], STAT[:, :, 1], STAT[:, :, 2], ALU.subtract, inc=(vS, 1))
        SE.wait(vS)
        SE.activation(STAT[:, :, 1], STAT[:, :, 1], AF.Ln, bias=EPSC[0:1, :])
        SE.activation(STAT[:, :, 1], STAT[:, :, 1], AF.Exp, scale=-0.5, inc=(cS, 1))
        VE.wait(cS)
        VE.tensor_tensor(STAT[:, :, 2], STAT[:, :, 0], STAT[:, :, 1], ALU.mult, inc=(vS, 1))
        VE.wait(vS)
        TE.wait(vS)
        pwait(P[6])
        TE.matmul(P[6][:, :], ROWS[:, 0, :], STAT[:, :, 1], start=True, stop=True)
        pwait(P[7])
        TE.matmul(P[7][:, :], ROWS[:, 0, :], STAT[:, :, 2], start=True, stop=True, inc=(tS, 1))
        VE.wait(tS)
        VE.tensor_copy(RSB[:, :], P[6][:, :])
        VE.tensor_copy(MRSB[:, :], P[7][:, :], inc=(vS, 1))
        pdone(P[6], vS)
        pdone(P[7], vS)
        GE.wait(vS)
        if outfree is not None:
            GE.wait(*outfree)
        pol = {}
        for dc in range(DC):
            if dc > 1:
                VE.wait(pS, pol[dc - 2])
            VE.tensor_tensor(TF[:, dc % 2, :], XB[:, dc, :], RSB[:, :], ALU.mult, inc=(vS, 1))
            GE.wait(vS)
            GE.tensor_tensor(out_buf[:, dc, osl], TF[:, dc % 2, :], MRSB[:, :], ALU.subtract,
                             inc=(pS, 1))
            pol[dc] = pS.v
        return pS.v

    def hidsl(hc):
        if hc < 32:
            return KRB[:, hc // 2, (hc % 2) * TB:(hc % 2) * TB + TB]
        h2 = hc - 32
        return VTB[:, h2 // 4, (h2 % 4) * TB:(h2 % 4) * TB + TB]

    def emit_ffn(lnsl, ln_ready, wmode, hid_free, scr_gate=None, xdst=None):
        slab = {}
        amm = {}
        relu_done = {}
        sgn = {}
        scrw = {}
        NB = 4
        for step in range(HC + 1):
            if step < HC:
                hc = step
                b = hc % NB
                if wmode == "fp32":
                    if hc >= 1:
                        GE.wait(cS, sgn[hc - 1])
                    GE.dma_start(out=WF[:, :], in_=up_h[hc, :, :, :].rearrange("p a b -> p (a b)"), inc=(dWG, 16))
                    SE.wait(dWG)
                    SE.wait(*wfb_busy[b])
                    if hc >= NB:
                        SE.wait(dWG, scrw[hc - NB])
                    SE.activation(WFB[:, b, :], WF[:, :], AF.Sign, inc=(cS, 1))
                    sgn[hc] = cS.v
                    GE.wait(cS, sgn[hc])
                    GE.dma_start(out=upsgn_h[hc, :, :, :],
                                 in_=WFB[:, b, :].rearrange("p (a c) -> p a c", a=DC), inc=(dWG, 16))
                    scrw[hc] = dWG.v
                    slab[hc] = (cS, sgn[hc])
                else:
                    if hc == 0 and scr_gate is not None:
                        SY.wait(*scr_gate)
                    SY.wait(*wfb_busy[b])
                    SY.dma_start(out=WFB[:, b, :].rearrange("p (a c) -> p a c", a=DC),
                                 in_=upsgn_h[hc, :, :, :], inc=(dWS, 16))
                    slab[hc] = (dWS, dWS.v)
            if step >= 1:
                h = step - 1
                b = h % NB
                pb = P[h % 2]
                ssem, scnt = slab[h]
                TE.wait(ssem, scnt)
                if h == 0:
                    TE.wait(pS, ln_ready)
                    if hid_free is not None:
                        SE.wait(*hid_free)
                if h >= 2:
                    TE.wait(cS, relu_done[h - 2])
                else:
                    pwait(pb)
                for dc in range(DC):
                    TE.matmul(pb[:, :], WFB[:, b, dc * 128:dc * 128 + 128], LNB[:, dc, lnsl],
                              start=(dc == 0), stop=(dc == DC - 1),
                              inc=(tS, 1) if dc == DC - 1 else None)
                amm[h] = tS.v
                wfb_busy[b] = (tS, amm[h])
                SE.wait(tS, amm[h])
                SE.activation(hidsl(h), pb[:, :], AF.Relu, scale=UPS[:, h:h + 1], inc=(cS, 1))
                relu_done[h] = cS.v
                pdone(pb, cS)
        a_done = cS.v
        bmm = {}
        bsgn = {}
        bscrw = {}
        last_res = 0
        NB = 4
        for dcg in range(8):
            pwait(P[2])
            pwait(P[3])
            for hg8 in range(8):
                gi = dcg * 8 + hg8
                b = gi % NB
                if wmode == "fp32":
                    if gi >= 1:
                        GE.wait(cS, bsgn[gi - 1])
                    GE.dma_start(out=WF[:, :], in_=dn_h[dcg, :, hg8 * 8:hg8 * 8 + 8, :].rearrange("p a b -> p (a b)"),
                                 inc=(dWG, 16))
                    SE.wait(dWG)
                    SE.wait(*wfb_busy[b])
                    if gi >= NB:
                        SE.wait(dWG, bscrw[gi - NB])
                    SE.activation(WFB[:, b, :], WF[:, :], AF.Sign, inc=(cS, 1))
                    bsgn[gi] = cS.v
                    GE.wait(cS, bsgn[gi])
                    GE.dma_start(out=dnsgn_h[dcg, :, hg8 * 8:hg8 * 8 + 8, :],
                                 in_=WFB[:, b, :].rearrange("p (a c) -> p a c", a=8), inc=(dWG, 16))
                    bscrw[gi] = dWG.v
                    bload = (cS, bsgn[gi])
                else:
                    if gi == 0 and scr_gate is not None:
                        SY.wait(*scr_gate)
                    SY.wait(*wfb_busy[b])
                    SY.dma_start(out=WFB[:, b, :].rearrange("p (a c) -> p a c", a=8),
                                 in_=dnsgn_h[dcg, :, hg8 * 8:hg8 * 8 + 8, :], inc=(dWS, 16))
                    bload = (dWS, dWS.v)
                TE.wait(bload[0], bload[1])
                if gi == 0:
                    TE.wait(cS, a_done)
                for hloc in range(8):
                    hc = hg8 * 8 + hloc
                    st = (hg8 == 0 and hloc == 0)
                    sp = (hg8 == 7 and hloc == 7)
                    TE.matmul(P[2][:, :], WFB[:, b, hloc * 256:hloc * 256 + 128], hidsl(hc),
                              start=st, stop=sp)
                    TE.matmul(P[3][:, :], WFB[:, b, hloc * 256 + 128:hloc * 256 + 256], hidsl(hc),
                              start=st, stop=sp, inc=(tS, 1) if hloc == 7 else None)
                bmm[gi] = tS.v
                wfb_busy[b] = (tS, bmm[gi])
            SE.wait(tS, bmm[dcg * 8 + 7])
            SE.wait(cS, CMB_DONE)
            if dcg >= 1:
                SE.wait(vS, last_res)
            SE.activation(TD[:, :], P[2][:, :], AF.Copy, scale=CMB[:, 2 * dcg:2 * dcg + 1],
                          inc=(cS, 1))
            VE.wait(cS)
            VE.tensor_tensor(XB[:, 2 * dcg, :], XB[:, 2 * dcg, :], TD[:, :], ALU.add, inc=(vS, 1))
            SE.wait(vS)
            SE.activation(TD[:, :], P[3][:, :], AF.Copy, scale=CMB[:, 2 * dcg + 1:2 * dcg + 2],
                          inc=(cS, 1))
            pdone(P[2], cS)
            pdone(P[3], cS)
            VE.wait(cS)
            VE.tensor_tensor(XB[:, 2 * dcg + 1, :], XB[:, 2 * dcg + 1, :], TD[:, :], ALU.add,
                             inc=(vS, 1))
            last_res = vS.v
            if xdst is not None:
                SY.wait(vS, last_res)
                SY.dma_start(out=xdst[:, 2 * dcg:2 * dcg + 2, :],
                             in_=XB[:, 2 * dcg:2 * dcg + 2, :], inc=(dIO, 16))
        return vS.v

    def emit_proj(out_buf, osl, lnsl, ln_ready, wsrc, rope, qoff, cs_ready, scr_gate=None):
        wload = {}
        pmm = {}
        emarks = {}
        ropemm = {}
        NB = 4
        for step in range(DC + 1):
            if step < DC:
                mc = step
                b = mc % NB
                if wsrc[0] == "cast":
                    GE.wait(*wfb_busy[b])
                    GE.dma_start(out=WFB[:, b, :].rearrange("p (a c) -> p a c", a=DC),
                                 in_=wsrc[1][mc, :, :, :], inc=(dWG, 16))
                    if wsrc[3]:
                        GE.wait(dWG)
                        GE.dma_start(out=wsrc[2][mc, :, :, :],
                                     in_=WFB[:, b, :].rearrange("p (a c) -> p a c", a=DC),
                                     inc=(dWG, 16))
                    wload[mc] = (dWG, dWG.v)
                else:
                    if mc == 0 and scr_gate is not None:
                        SY.wait(*scr_gate)
                    SY.wait(*wfb_busy[b])
                    SY.dma_start(out=WFB[:, b, :].rearrange("p (a c) -> p a c", a=DC),
                                 in_=wsrc[1][mc, :, :, :], inc=(dWS, 16))
                    wload[mc] = (dWS, dWS.v)
            if step >= 1:
                m = step - 1
                b = m % NB
                pb = P[m % 2]
                TE.wait(wload[m][0], wload[m][1])
                if m == 0:
                    TE.wait(pS, ln_ready)
                    if cs_ready is not None:
                        VE.wait(*cs_ready)
                        GE.wait(*cs_ready)
                if m >= 2:
                    TE.wait(cS, emarks[m - 2])
                else:
                    pwait(pb)
                for dc in range(DC):
                    TE.matmul(pb[:, :], WFB[:, b, dc * 128:dc * 128 + 128], LNB[:, dc, lnsl],
                              start=(dc == 0), stop=(dc == DC - 1),
                              inc=(tS, 1) if dc == DC - 1 else None)
                pmm[m] = tS.v
                wfb_busy[b] = (tS, pmm[m])
                if rope is None:
                    SE.wait(tS, pmm[m])
                    SE.activation(out_buf[:, m, osl], pb[:, :], AF.Copy, inc=(cS, 1))
                    emarks[m] = cS.v
                    pdone(pb, cS)
                else:
                    SE.wait(tS, pmm[m])
                    if m >= 1:
                        SE.wait(tS, ropemm[m - 1])
                    SE.activation(KPRE, pb[:, :], AF.Copy, inc=(cS, 1))
                    emarks[m] = cS.v
                    pdone(pb, cS)
                    VE.wait(cS, emarks[m])
                    GE.wait(cS, emarks[m])
                    GE.tensor_tensor(T1, KPRE, CSB[:, 1, qoff:qoff + TB], ALU.mult,
                                     inc=(pS, 1))
                    VE.tensor_tensor(T2, KPRE, CSB[:, 0, qoff:qoff + TB], ALU.mult,
                                     inc=(vS, 1))
                    TE.wait(pS)
                    TE.wait(vS)
                    pwait(P[4])
                    TE.matmul(P[4][:, :], PRM[:, :], T1, start=True, stop=False)
                    TE.matmul(P[4][:, :], IDB[:, :], T2, start=False, stop=True, inc=(tS, 1))
                    ropemm[m] = tS.v
                    SE.wait(tS, ropemm[m])
                    SE.activation(out_buf[:, m, osl], P[4][:, :], AF.Copy, inc=(cS, 1))
                    pdone(P[4], cS)
        return cS.v, tS.v

    def emit_vproj(lnsl, ln_ready, wsrc, tb, scr_gate=None):
        wload = {}
        pmm = {}
        evd = {}
        trd = {}
        NB = 4
        for step in range(DC + 1):
            if step < DC:
                mc = step
                b = mc % NB
                if wsrc[0] == "cast":
                    GE.wait(*wfb_busy[b])
                    GE.dma_start(out=WFB[:, b, :].rearrange("p (a c) -> p a c", a=DC),
                                 in_=wsrc[1][mc, :, :, :], inc=(dWG, 16))
                    if wsrc[3]:
                        GE.wait(dWG)
                        GE.dma_start(out=wsrc[2][mc, :, :, :],
                                     in_=WFB[:, b, :].rearrange("p (a c) -> p a c", a=DC),
                                     inc=(dWG, 16))
                    wload[mc] = (dWG, dWG.v)
                else:
                    if mc == 0 and scr_gate is not None:
                        SY.wait(*scr_gate)
                    SY.wait(*wfb_busy[b])
                    SY.dma_start(out=WFB[:, b, :].rearrange("p (a c) -> p a c", a=DC),
                                 in_=wsrc[1][mc, :, :, :], inc=(dWS, 16))
                    wload[mc] = (dWS, dWS.v)
            if step >= 1:
                m = step - 1
                b = m % NB
                pb = P[m % 2]
                TE.wait(wload[m][0], wload[m][1])
                if m == 0:
                    TE.wait(pS, ln_ready)
                if m >= 2:
                    TE.wait(vS, evd[m - 2])
                else:
                    pwait(pb)
                for dc in range(DC):
                    TE.matmul(pb[:, :], WFB[:, b, dc * 128:dc * 128 + 128], LNB[:, dc, lnsl],
                              start=(dc == 0), stop=(dc == DC - 1),
                              inc=(tS, 1) if dc == DC - 1 else None)
                pmm[m] = tS.v
                wfb_busy[b] = (tS, pmm[m])
                VE.wait(tS, pmm[m])
                if m >= 1:
                    VE.wait(tS, trd[m - 1])
                VE.tensor_copy(TF[:, m % 2, :], pb[:, :], inc=(vS, 1))
                evd[m] = vS.v
                pdone(pb, vS)
                TE.wait(vS, evd[m])
                for i in range(4):
                    pt = P[4 + (i % 2)]
                    pwait(pt)
                    TE.transpose(pt[:, 0:128], TF[:, m % 2, i * 128:i * 128 + 128], IDN[:, :],
                                 inc=(tS, 1))
                    SE.wait(tS)
                    SE.activation(VTB[:, tb * 4 + i, m * 128:m * 128 + 128], pt[:, 0:128],
                                  AF.Copy, inc=(cS, 1))
                    pdone(pt, cS)
                trd[m] = tS.v
        return cS.v

    def emit_norms(buf, ncols, col, ready):
        SE.wait(*ready)
        for h in range(NH):
            SE.activation(KSQ[:, 0:ncols], buf[:, h, 0:ncols], AF.Square, inc=(cS, 1))
            TE.wait(cS)
            pwait(P[6])
            TE.matmul(P[6][0:1, :], ONEC[:, :], KSQ[:, 0:TB], start=True, stop=True)
            pwait(P[7])
            TE.matmul(P[7][0:1, :], ONEC[:, :], KSQ[:, TB:2 * TB], start=True, stop=True,
                      inc=(tS, 1))
            VE.wait(tS)
            VE.reduce_max(KNT[:, 0:1], P[6][0:1, :], axis=AX.X, inc=(vS, 1))
            VE.reduce_max(KNT[:, 1:2], P[7][0:1, :], axis=AX.X, inc=(vS, 1))
            pdone(P[6], vS)
            pdone(P[7], vS)
            VE.wait(vS)
            VE.tensor_tensor(KNT[:, 0:1], KNT[:, 0:1], KNT[:, 1:2], ALU.max, inc=(vS, 1))
            VE.wait(vS)
            VE.tensor_tensor(NRM[:, col, h:h + 1], NRM[:, col, h:h + 1], KNT[:, 0:1], ALU.max,
                             inc=(vS, 1))
        return vS.v

    def arm_info(arm):
        log = physical_to_logical_core_id(int(arm) % 128, arm.node_id_4,
                                          container_num_cores=8) % 8
        return log, log % 2

    regs = {}

    xb_free_v = 0
    xb_free_io = 0
    lnb_free = {0: (tS, 0), 1: (tS, 0)}
    hid_free = None
    ffn_scr = None
    k_scr = None
    v_scr = None
    TBSL = [slice(0, TB), slice(TB, 2 * TB)]

    for slot in range(COV):
        SY.wait(pS)
        SY.wait(vS)
        SY.dma_start(out=CSB[:, 0, :], in_=cscov_h[slot, 0, :, :], inc=(dIO, 16))
        SY.dma_start(out=CSB[:, 1, :], in_=cscov_h[slot, 1, :, :], inc=(dIO, 16))
        cs_ready = (dIO, dIO.v)
        for tb in range(2):
            wmode = "fp32" if (slot == 0 and tb == 0) else "bf16"
            SY.wait(vS, xb_free_v)
            SY.wait(dIO, xb_free_io)
            SY.dma_start(out=XB[:, :, :], in_=xcov_h[slot, :, :, TBSL[tb]], inc=(dIO, 16))
            xready = (dIO, dIO.v)
            lnr = emit_ln(LNB, TBSL[tb], xready, lnb_free[tb])
            res = emit_ffn(TBSL[tb], lnr, wmode, hid_free, scr_gate=ffn_scr,
                           xdst=xmid_h[slot, :, :, TBSL[tb]])
            if slot == 0 and tb == 0:
                ffn_scr = (dWG, dWG.v)
            hid_free = None
            lnb_free[tb] = (tS, tS.v)
            xb_free_v = res
            xb_free_io = dIO.v
        for tb in range(2):
            SY.wait(vS, xb_free_v)
            SY.wait(dIO, xb_free_io)
            SY.dma_start(out=XB[:, :, :], in_=xmid_h[slot, :, :, TBSL[tb]], inc=(dIO, 16))
            xready = (dIO, dIO.v)
            lnr = emit_ln(LNB, TBSL[tb], xready, lnb_free[tb])
            ksrc = ("cast", wk_h, wkbf_h, True) if (slot == 0 and tb == 0) else ("scr", wkbf_h)
            kdone, kmm = emit_proj(KRB, TBSL[tb], TBSL[tb], lnr, ksrc, "rope", tb * TB,
                                   cs_ready, scr_gate=k_scr)
            if slot == 0 and tb == 0:
                k_scr = (dWG, dWG.v)
            vsrc = ("cast", wv_h, wvbf_h, True) if (slot == 0 and tb == 0) else ("scr", wvbf_h)
            vdone = emit_vproj(TBSL[tb], lnr, vsrc, tb, scr_gate=v_scr)
            if slot == 0 and tb == 0:
                v_scr = (dWG, dWG.v)
            lnb_free[tb] = (tS, tS.v)
            xb_free_v = vS.v
        emit_norms(KRB, TPC, 0, (cS, cS.v))
        SY.wait(cS)
        SY.wait(vS)
        kvw = []
        for hq in range(4):
            dSH.v += 16
        for tt in range(8):
            dSH.v += 16
        pre = dSH.v

        def mk_kv_writer(slotc, target):
            def go(sy):
                for arm in nc.switch_core_id([nc.sync], n=128):
                    log, par = arm_info(arm)
                    g = 2 * slotc + par
                    for hq in range(4):
                        sy.dma_start(
                            out=shk_h[:, hq * 4:hq * 4 + 4, g * 1024:(g + 1) * 1024],
                            in_=KRB[:, hq * 4:hq * 4 + 4, :],
                        ).then_inc(target, 16)
                    for tt in range(8):
                        sy.dma_start(
                            out=shv_h[g * 1024 + tt * 128:g * 1024 + (tt + 1) * 128, :],
                            in_=VTB[:, tt, :],
                        ).then_inc(target, 16)
            return go

        SY.raw(mk_kv_writer(slot, dSH.h))
        hid_free = (dSH, dSH.v)

    knorm_ready = vS.v
    SY.wait(vS, knorm_ready)
    all_sh = dSH.v
    dSH.v += 16
    dSH.v += 16
    dSH.v += 16
    BAR_END = dSH.v

    def barrier(sy):
        r_mg = sy.register("r_mg").__enter__()
        r_pf = sy.register("r_pf").__enter__()
        r_nd = sy.register("r_nd").__enter__()
        regs["mg"], regs["pf"], regs["nd"] = r_mg, r_pf, r_nd
        for arm in nc.switch_core_id([nc.sync], n=128):
            log, par = arm_info(arm)
            other = 1 - par
            sy.dma_start(out=shkn_h[par:par + 1, :], in_=NRM[:, 0, :]).then_inc(dSH.h, 16)
            sy.wait_ge(dSH.h, all_sh + 16)
            sy.dma_start(out=shfl_h[par:par + 1, :], in_=MGC[:, :]).then_inc(dSH.h, 16)
            sy.wait_ge(dSH.h, all_sh + 32)
            sy.load(r_mg, MGC[0:1, 0:1])
            sy.reg_mov(r_nd, 1)
            with sy.While(r_nd):
                sy.load(r_pf, shfl_h[other:other + 1, 0:1])
                sy.reg_alu(r_nd, r_pf, r_mg, ALU.not_equal)
            sy.dma_start(out=PKN[:, :], in_=shkn_h[other:other + 1, :]).then_inc(dSH.h, 16)

    SY.raw(barrier)
    pkn_ready = BAR_END

    SY.wait(pS)
    SY.wait(vS)
    SY.dma_start(out=CSB[:, 0, :], in_=csown_h[0, :, :], inc=(dIO, 16))
    SY.dma_start(out=CSB[:, 1, :], in_=csown_h[1, :, :], inc=(dIO, 16))
    cs_own = (dIO, dIO.v)

    qdone = 0
    for tb in range(2):
        SY.wait(vS, xb_free_v)
        SY.wait(dIO, xb_free_io)
        dIO.v += 16

        def mk_xmid_rd(tbc, target, cnt):
            def go(sy):
                for arm in nc.switch_core_id([nc.sync], n=128):
                    log, par = arm_info(arm)
                    myslot = log // 2
                    sy.dma_start(out=XB[:, :, :],
                                 in_=xmid_h[myslot, :, :, TBSL[tbc]]).then_inc(target, 16)
                sy.wait_ge(target, cnt)
            return go

        SY.raw(mk_xmid_rd(tb, dIO.h, dIO.v))
        xready = (dIO, dIO.v)
        lnr = emit_ln(LNB, TBSL[tb], xready, lnb_free[tb])
        qsrc = ("cast", wq_h, None, False)
        qdone, _ = emit_proj(KRB, TBSL[tb], TBSL[tb], lnr, qsrc, "rope", tb * TB, cs_own)
        lnb_free[tb] = (tS, tS.v)
        xb_free_v = vS.v
    emit_norms(KRB, TPC, 1, (cS, qdone))

    VE.wait(dSH, pkn_ready)
    VE.wait(vS)
    VE.tensor_tensor(NRM[:, 2, :], NRM[:, 0, :], PKN[:, :], ALU.max, inc=(vS, 1))
    VE.wait(vS)
    VE.tensor_tensor(NRM[:, 2, :], NRM[:, 2, :], NRM[:, 1, :], ALU.mult, inc=(vS, 1))
    VE.wait(vS)
    VE.tensor_scalar_max(NRM[:, 2, :], NRM[:, 2, :], 1e-6, inc=(vS, 1))
    SE.wait(vS)
    SE.activation(NRM[:, 3, :], NRM[:, 2, :], AF.Sqrt, scale=1.0 / 128.0, inc=(cS, 1))
    VE.wait(cS)
    VE.tensor_scalar(NRM[:, 3, :], NRM[:, 3, :], 1.01, 0.5, ALU.mult, op1=ALU.add, inc=(vS, 1))
    TE.wait(vS)
    pwait(P[6])
    TE.matmul(P[6][:, 0:16], ROWS[:, 1, :], NRM[:, 3, :], start=True, stop=True, inc=(tS, 1))
    VE.wait(tS)
    VE.tensor_copy(NEGM[:, :], P[6][:, 0:16], inc=(vS, 1))
    pdone(P[6], vS)
    negm_ready = vS.v

    q_ready = qdone

    exp_done = {}
    pv_mm = {}
    acc_done = {}
    scr_free = {}
    atn_marks = []
    for qh in range(2):
        qsl = slice(qh * TB, qh * TB + TB)
        for (h0, h1) in HGROUPS:
            g = h1 - h0
            kready = {}
            vready = {}
            kcons = {}
            pvdone = {}

            def kload(j):
                if j >= 2:
                    SY.wait(tS, kcons[j - 2])
                for hi in range(g):
                    SY.dma_start(out=KSL[:, j % 2, hi, :],
                                 in_=shk_h[:, h0 + hi, j * 256:(j + 1) * 256],
                                 inc=(dKV, 16))
                kready[j] = dKV.v

            def vload(kc):
                if kc >= 2:
                    SY.wait(tS, pvdone[kc - 2])
                SY.dma_start(out=VSL[:, kc % 2, 0:g * 128],
                             in_=shv_h[kc * 128:(kc + 1) * 128, h0 * 128:h1 * 128],
                             inc=(dKV, 16))
                vready[kc] = dKV.v

            SY.wait(dSH, pkn_ready)
            kload(0)
            kload(1)
            vload(0)
            vload(1)
            it = 0
            pending = []

            def flush_pv(keep):
                while len(pending) > keep:
                    hi2, kc2, it2 = pending.pop(0)
                    TE.wait(cS, exp_done[it2])
                    if kc2 == 0:
                        pwait(P[hi2])
                    TE.matmul(P[hi2][:, :], VSL[:, kc2 % 2, hi2 * 128:(hi2 + 1) * 128],
                              EXB[:, it2 % 4, :], start=(kc2 == 0), stop=(kc2 == 63),
                              inc=(tS, 1))
                    pv_mm[it2] = tS.v
                    if hi2 == g - 1:
                        pvdone[kc2] = tS.v
                        if kc2 + 2 < 64:
                            vload(kc2 + 2)

            for j in range(32):
                TE.wait(dKV, kready[j])
                for tt in range(2):
                    kc = j * 2 + tt
                    TE.wait(dKV, vready[kc])
                    for hi in range(g):
                        h = h0 + hi
                        sbk = P[6] if it % 2 == 0 else P[7]
                        if it < 2:
                            pwait(sbk)
                        else:
                            TE.wait(cS, exp_done[it - 2])
                        if it == 0:
                            TE.wait(cS, q_ready)
                            TE.wait(vS, negm_ready)
                        TE.matmul(sbk[:, :], KSL[:, j % 2, hi, tt * 128:(tt + 1) * 128],
                                  KRB[:, h, qsl], start=True, stop=True, inc=(tS, 1))
                        smm = tS.v
                        SE.wait(tS, smm)
                        if it >= 4:
                            SE.wait(vS, acc_done[it - 4])
                        SE.activation(EXB[:, it % 4, :], sbk[:, :], AF.Exp,
                                      bias=NEGM[:, h:h + 1], scale=INVSQHD, inc=(cS, 1))
                        exp_done[it] = cS.v
                        VE.wait(cS, exp_done[it])
                        if kc == 0:
                            VE.tensor_copy(ACC[:, hi, :], EXB[:, it % 4, :], inc=(vS, 1))
                        else:
                            VE.tensor_tensor(ACC[:, hi, :], ACC[:, hi, :], EXB[:, it % 4, :],
                                             ALU.add, inc=(vS, 1))
                        acc_done[it] = vS.v
                        pending.append((hi, kc, it))
                        flush_pv(2)
                        it += 1
                kcons[j] = tS.v
                if j + 2 < 32:
                    kload(j + 2)
            flush_pv(0)
            last_pv = tS.v
            for hi in range(g):
                h = h0 + hi
                VE.wait(tS, last_pv)
                VE.tensor_copy(T1, ACC[:, hi, :], inc=(vS, 1))
                TE.wait(vS)
                pwait(P[6])
                TE.matmul(P[6][0:1, :], ONEC[:, :], T1, start=True, stop=True, inc=(tS, 1))
                VE.wait(tS)
                VE.reciprocal(STAT[:, :, 0], P[6][0:1, :], inc=(vS, 1))
                pdone(P[6], vS)
                TE.wait(vS)
                pwait(P[7])
                TE.matmul(P[7][:, :], ROWS[:, 0, :], STAT[:, :, 0], start=True, stop=True,
                          inc=(tS, 1))
                VE.wait(tS)
                VE.tensor_copy(RSB[:, :], P[7][:, :], inc=(vS, 1))
                pdone(P[7], vS)
                VE.tensor_tensor(ATN[:, h * 1024 + qh * TB:h * 1024 + qh * TB + TB],
                                 P[hi][:, :], RSB[:, :], ALU.mult, inc=(vS, 1))
                pdone(P[hi], vS)
            atn_marks.append(vS.v)
    atn_ready = vS.v

    for tb in range(2):
        SY.wait(vS, xb_free_v)
        SY.wait(dIO, xb_free_io)
        dIO.v += 16
        SY.raw(mk_xmid_rd(tb, dIO.h, dIO.v))
        xready = (dIO, dIO.v)
        owload = {}
        omm = {}
        oev = {}
        NB = 4
        for step in range(DC + 1):
            if step < DC:
                mc = step
                b = mc % NB
                GE.wait(*wfb_busy[b])
                GE.dma_start(out=WFB[:, b, :].rearrange("p (a c) -> p a c", a=DC),
                             in_=wo_h[mc, :, :, :], inc=(dWG, 16))
                owload[mc] = (dWG, dWG.v)
            if step >= 1:
                m = step - 1
                b = m % NB
                pb = P[m % 2]
                TE.wait(owload[m][0], owload[m][1])
                if m == 0:
                    TE.wait(vS, atn_ready)
                if m >= 2:
                    TE.wait(vS, oev[m - 2])
                else:
                    pwait(pb)
                for dc in range(DC):
                    TE.matmul(pb[:, :], WFB[:, b, dc * 128:dc * 128 + 128],
                              ATN[:, dc * 1024 + tb * TB:dc * 1024 + tb * TB + TB],
                              start=(dc == 0), stop=(dc == DC - 1),
                              inc=(tS, 1) if dc == DC - 1 else None)
                omm[m] = tS.v
                wfb_busy[b] = (tS, omm[m])
                VE.wait(tS, omm[m])
                if m == 0:
                    VE.wait(*xready)
                VE.tensor_tensor(XB[:, m, :], XB[:, m, :], pb[:, :], ALU.add, inc=(vS, 1))
                oev[m] = vS.v
                pdone(pb, vS)
        SY.wait(vS, oev[DC - 1])
        SY.dma_start(out=x1_h[:, :, TBSL[tb]], in_=XB[:, :, :], inc=(dIO, 16))
        xb_free_v = vS.v
        xb_free_io = dIO.v
    o_done = tS.v

    hid_free = (tS, o_done)
    for tb in range(2):
        SY.wait(vS, xb_free_v)
        SY.wait(dIO, xb_free_io)
        SY.dma_start(out=XB[:, :, :], in_=x1_h[:, :, TBSL[tb]], inc=(dIO, 16))
        xready = (dIO, dIO.v)
        lnr = emit_ln(LNB, TBSL[tb], xready, lnb_free[tb])
        res = emit_ffn(TBSL[tb], lnr, "bf16", hid_free, scr_gate=ffn_scr)
        hid_free = None
        lnb_free[tb] = (tS, tS.v)
        SY.wait(vS, res)
        SY.dma_start(out=out_h[:, :, TBSL[tb]], in_=XB[:, :, :], inc=(dIO, 16))
        xb_free_v = res
        xb_free_io = dIO.v

    dIO.v += 16

    def resetter(sy):
        for arm in nc.switch_core_id([nc.sync], n=128):
            log, par = arm_info(arm)
            sy.dma_start(out=shfl_h[par:par + 1, :], in_=ZRO[:, :]).then_inc(dIO.h, 16)

    SY.raw(resetter)
    SY.wait(dIO)
    SY.wait(dSH)
    SY.wait(dKV)

    with nc.Block() as block:
        def replay(engine_name):
            def body(eng):
                for (en, op, a, k, inc) in script:
                    if en != engine_name:
                        continue
                    if op is None:
                        a[0](eng)
                        continue
                    inst = getattr(eng, op)(*a, **k)
                    if inc is not None:
                        inst.then_inc(inc[0], inc[1])
            return body

        block.tensor(replay("tensor"))
        block.vector(replay("vector"))
        block.scalar(replay("scalar"))
        block.gpsimd(replay("gpsimd"))
        block.sync(replay("sync"))

    for t in reversed(ctx):
        t.__exit__(None, None, None)
    return nc


_CACHE = {}


def _stage_weights(L0_attn_wq, L0_attn_wk, L0_attn_wv, L0_attn_wo,
                   L0_ffn_up_weight, L0_ffn_down_weight,
                   L0_ffn_up_scale, L0_ffn_down_scale, L0_ffn_output_scale):
    st = {}

    def wstage(w):
        return np.ascontiguousarray(w.reshape(DC, 128, DC, 128).transpose(0, 3, 2, 1))

    st["wqt"] = wstage(L0_attn_wq)
    st["wkt"] = wstage(L0_attn_wk)
    st["wvt"] = wstage(L0_attn_wv)
    st["wot"] = wstage(L0_attn_wo)
    st["upt"] = np.ascontiguousarray(
        L0_ffn_up_weight.reshape(HC, 128, DC, 128).transpose(0, 3, 2, 1))
    dw = L0_ffn_down_weight.reshape(8, 2, 128, HC, 128)
    st["dnt"] = np.ascontiguousarray(
        dw.transpose(0, 4, 3, 1, 2).reshape(8, 128, HC, 256))
    st["upsc"] = np.ascontiguousarray(L0_ffn_up_scale.reshape(HC, 128).T)
    st["dnsc"] = np.ascontiguousarray(L0_ffn_down_scale.reshape(DC, 128).T)
    st["osc"] = np.ascontiguousarray(L0_ffn_output_scale.reshape(1, 1))
    return st


def _consts():
    cf = np.zeros((128, 648), np.float32)
    prm = np.zeros((128, 128), np.float32)
    for i in range(64):
        prm[2 * i + 1, 2 * i] = 1.0
        prm[2 * i, 2 * i + 1] = 1.0
    cf[:, 0:128] = prm
    cf[:, 128:256] = np.eye(128, dtype=np.float32)
    cf[:, 256] = 1.0
    cf[0, 257:385] = 1.0
    cf[0, 385:513] = -1.0
    cf[0, 513:641] = 0.5
    return cf


def _cs_tables():
    import ml_dtypes
    hd = 128
    scale = np.arange(0, hd, 2, dtype=np.float64) / hd
    inv = 1.0 / 10000.0 ** scale
    t = np.arange(SEQ, dtype=np.float64)
    ang = np.outer(inv, t)
    C = np.empty((128, SEQ), np.float64)
    S2 = np.empty((128, SEQ), np.float64)
    C[0::2] = np.cos(ang)
    C[1::2] = np.cos(ang)
    S2[0::2] = np.sin(ang)
    S2[1::2] = -np.sin(ang)
    return (C.astype(ml_dtypes.bfloat16), S2.astype(ml_dtypes.bfloat16))


def kernel(L0_attn_x, L0_attn_wq, L0_attn_wk, L0_attn_wv, L0_attn_wo,
           L0_ffn_up_weight, L0_ffn_down_weight, L0_ffn_up_scale,
           L0_ffn_down_scale, L0_ffn_output_scale):
    import ml_dtypes
    if "nc" not in _CACHE:
        _CACHE["nc"] = build_program()
    nc = _CACHE["nc"]

    st = _stage_weights(np.asarray(L0_attn_wq, np.float32), np.asarray(L0_attn_wk, np.float32),
                        np.asarray(L0_attn_wv, np.float32), np.asarray(L0_attn_wo, np.float32),
                        np.asarray(L0_ffn_up_weight, np.float32),
                        np.asarray(L0_ffn_down_weight, np.float32),
                        np.asarray(L0_ffn_up_scale, np.float32),
                        np.asarray(L0_ffn_down_scale, np.float32),
                        np.asarray(L0_ffn_output_scale, np.float32))
    cf = _consts()
    C, S2 = _cs_tables()
    x = np.asarray(L0_attn_x, np.float32).reshape(SEQ, D)
    magic = np.random.randint(1, 2 ** 31 - 1, size=(1, NMAGIC)).astype(np.int32)

    def fm(chunk):
        blk = x[chunk * 1024:(chunk + 1) * 1024, :]
        return np.ascontiguousarray(blk.T.reshape(DC, 128, 1024).transpose(1, 0, 2))

    in_maps = []
    for c in range(8):
        par = c % 2
        chunks = [par, par + 2, par + 4, par + 6]
        xcov = np.stack([fm(g) for g in chunks], axis=0)
        cscov = np.stack(
            [np.stack([C[:, g * 1024:(g + 1) * 1024], S2[:, g * 1024:(g + 1) * 1024]], axis=0)
             for g in chunks], axis=0)
        csown = np.stack([C[:, c * 1024:(c + 1) * 1024], S2[:, c * 1024:(c + 1) * 1024]], axis=0)
        m = dict(st)
        m["xcov"] = xcov
        m["cscov"] = np.ascontiguousarray(cscov)
        m["csown"] = np.ascontiguousarray(csown)
        m["constf"] = cf
        m["magic"] = magic
        in_maps.append(m)

    res = run_bass_kernel_spmd(nc, in_maps, core_ids=list(range(8)))
    out = np.empty((SEQ, D), np.float32)
    for c in range(8):
        arr = res.results[c]["out"]
        out[c * 1024:(c + 1) * 1024, :] = arr.transpose(1, 0, 2).reshape(D, 1024).T
    return out.reshape(1, SEQ, D)


# version 06
# speedup vs baseline: 1.2475x; 1.1194x over previous
import numpy as np

import concourse.bass as bass
import concourse.mybir as mybir
from concourse.bass_utils import run_bass_kernel_spmd
from concourse.libnrt import physical_to_logical_core_id

F32 = mybir.dt.float32
BF16 = mybir.dt.bfloat16
I32 = mybir.dt.int32
AF = mybir.ActivationFunctionType
ALU = mybir.AluOpType
AX = mybir.AxisListType

D = 2048
HID = 8192
SEQ = 8192
NH = 16
TPC = 1024
COV = 4
TB = 512
DC = 16
HC = 64
NMAGIC = 4
INVSQHD = float(1.0 / np.sqrt(128.0))
EPS = 1e-5
HGROUPS = [(0, 4), (4, 8), (8, 12), (12, 16)]


class Sem:
    def __init__(self, handle):
        self.h = handle
        self.v = 0


class Rec:

    def __init__(self, script, name):
        self.__dict__["_s"] = script
        self.__dict__["_n"] = name
        self.__dict__["_seen"] = {}

    def __getattr__(self, op):
        def rec(*a, inc=None, **k):
            pair = None
            if inc is not None:
                sem, n = inc
                sem.v += n
                pair = (sem.h, n)
            self._s.append((self._n, op, a, k, pair))
        return rec

    def wait(self, sem, v=None):
        t = sem.v if v is None else v
        if t <= 0:
            return
        if self._seen.get(id(sem), -1) >= t:
            return
        self._seen[id(sem)] = t
        self._s.append((self._n, "wait_ge", (sem.h, t), {}, None))

    def raw(self, fn):
        self._s.append((self._n, None, (fn,), {}, None))


def build_program():
    nc = bass.Bass("TRN2", target_bir_lowering=False)

    xcov_h = nc.dram_tensor("xcov", [COV, 128, DC, TPC], F32, kind="ExternalInput").ap()
    wq_h = nc.dram_tensor("wqt", [DC, 128, DC, 128], F32, kind="ExternalInput").ap()
    wk_h = nc.dram_tensor("wkt", [DC, 128, DC, 128], F32, kind="ExternalInput").ap()
    wv_h = nc.dram_tensor("wvt", [DC, 128, DC, 128], F32, kind="ExternalInput").ap()
    wo_h = nc.dram_tensor("wot", [DC, 128, DC, 128], F32, kind="ExternalInput").ap()
    up_h = nc.dram_tensor("upt", [HC, 128, DC, 128], F32, kind="ExternalInput").ap()
    dn_h = nc.dram_tensor("dnt", [8, 128, HC, 256], F32, kind="ExternalInput").ap()
    upsc_h = nc.dram_tensor("upsc", [128, HC], F32, kind="ExternalInput").ap()
    dnsc_h = nc.dram_tensor("dnsc", [128, DC], F32, kind="ExternalInput").ap()
    osc_h = nc.dram_tensor("osc", [1, 1], F32, kind="ExternalInput").ap()
    cscov_h = nc.dram_tensor("cscov", [COV, 2, 128, TPC], BF16, kind="ExternalInput").ap()
    csown_h = nc.dram_tensor("csown", [2, 128, TPC], BF16, kind="ExternalInput").ap()
    cf_h = nc.dram_tensor("constf", [128, 648], F32, kind="ExternalInput").ap()
    magic_h = nc.dram_tensor("magic", [1, NMAGIC], I32, kind="ExternalInput").ap()
    out_h = nc.dram_tensor("out", [128, DC, TPC], F32, kind="ExternalOutput").ap()

    upsgn_h = nc.dram_tensor("upsgn", [HC, 128, DC, 128], BF16, kind="Internal").ap()
    dnsgn_h = nc.dram_tensor("dnsgn", [8, 128, HC, 256], BF16, kind="Internal").ap()
    wkbf_h = nc.dram_tensor("wkbf", [DC, 128, DC, 128], BF16, kind="Internal").ap()
    wvbf_h = nc.dram_tensor("wvbf", [DC, 128, DC, 128], BF16, kind="Internal").ap()
    xmid_h = nc.dram_tensor("xmidscr", [COV, 128, DC, TPC], F32, kind="Internal").ap()
    x1_h = nc.dram_tensor("x1scr", [128, DC, TPC], F32, kind="Internal").ap()

    shk_h = nc.dram_tensor("shk", [128, NH, SEQ], BF16, kind="Internal", addr_space="Shared").ap()
    shv_h = nc.dram_tensor("shv", [SEQ, D], BF16, kind="Internal", addr_space="Shared").ap()
    shkn_h = nc.dram_tensor("shkn", [2, 16], F32, kind="Internal", addr_space="Shared").ap()
    shfl_h = nc.dram_tensor("shfl", [2, NMAGIC], I32, kind="Internal", addr_space="Shared").ap()

    ctx = []

    def enter(t):
        ctx.append(t)
        return t.__enter__()

    sbuf = lambda n, sh, dt: enter(nc.sbuf_tensor(n, sh, dt))
    psum = lambda n: enter(nc.psum_tensor(n, [128, 512], F32))

    XB = sbuf("XB", [128, DC, TB], F32)
    LNB = sbuf("LNB", [128, DC, TPC], BF16)
    KRB = sbuf("KRB", [128, NH, TPC], BF16)
    VTB = sbuf("VTB", [128, 8, D], BF16)
    WF = sbuf("WF", [128, 2048], F32)
    WFB = sbuf("WFB", [128, 4, 2048], BF16)
    KSL = sbuf("KSL", [128, 2, 6, 256], BF16)
    VSL = sbuf("VSL", [128, 2, 768], BF16)
    ACC = sbuf("ACC", [128, 6, TB], F32)
    EXB = sbuf("EXB", [128, 4, TB], BF16)
    KPRE = EXB[:, 2, :]
    KSQ = EXB[:, 0:2, :].rearrange("p a b -> p (a b)")
    T1 = EXB[:, 0, :]
    T2 = EXB[:, 1, :]
    TF = sbuf("TF", [128, 2, TB], F32)
    TD = sbuf("TD", [128, TB], F32)
    RSB = sbuf("RSB", [128, TB], F32)
    MRSB = sbuf("MRSB", [128, TB], F32)
    CSB = sbuf("CSB", [128, 2, TPC], BF16)
    XBF = sbuf("XBF", [128, 2, TB], BF16)
    XSQ = sbuf("XSQ", [128, 2, TB], BF16)
    STAT = sbuf("STAT", [1, TB, 3], F32)
    PRM = sbuf("PRM", [128, 128], BF16)
    IDB = sbuf("IDB", [128, 128], BF16)
    IDN = sbuf("IDN", [128, 128], F32)
    ONEC = sbuf("ONEC", [128, 1], BF16)
    ROWS = sbuf("ROWS", [1, 3, 128], F32)
    UPS = sbuf("UPS", [128, HC], F32)
    DNS = sbuf("DNS", [128, DC], F32)
    CMB = sbuf("CMB", [128, DC], F32)
    OSS = sbuf("OSS", [1, 1], F32)
    NRM = sbuf("NRM", [1, 4, 16], F32)
    KNT = sbuf("KNT", [1, 2], F32)
    NEGM = sbuf("NEGM", [128, 16], F32)
    EPSC = sbuf("EPSC", [128, 1], F32)
    MGC = sbuf("MGC", [1, NMAGIC], I32)
    ZRO = sbuf("ZRO", [1, NMAGIC], I32)
    PKN = sbuf("PKN", [1, 16], F32)

    P03 = [psum(f"pp{i}") for i in range(4)]
    PS45 = enter(nc.psum_tensor("pp45", [128, 1024], F32))
    PS67 = enter(nc.psum_tensor("pp67", [128, 1024], F32))
    P = P03 + [PS45[:, 0:512], PS45[:, 512:1024], PS67[:, 0:512], PS67[:, 512:1024]]
    ATN = VTB[:, :, :].rearrange("p a b -> p (a b)")

    sems = {}

    def mksem(n):
        s = Sem(enter(nc.semaphore(n)))
        sems[n] = s
        return s

    dIO = mksem("dIO")
    dSH = mksem("dSH")
    dKV = mksem("dKV")
    dWS = mksem("dWS")
    dWG = mksem("dWG")
    tS = mksem("tS")
    vS = mksem("vS")
    cS = mksem("cS")
    pS = mksem("pS")

    script = []
    TE = Rec(script, "tensor")
    VE = Rec(script, "vector")
    SE = Rec(script, "scalar")
    GE = Rec(script, "gpsimd")
    SY = Rec(script, "sync")

    bguard = {}
    wfb_busy = {0: (tS, 0), 1: (tS, 0), 2: (tS, 0), 3: (tS, 0)}

    def pwait(p):
        g = bguard.pop(id(p), None)
        if g is not None:
            TE.wait(g[0], g[1])

    def pdone(p, sem):
        bguard[id(p)] = (sem, sem.v)

    GE.dma_start(out=PRM[:, :], in_=cf_h[:, 0:128], inc=(dWG, 16))
    GE.dma_start(out=IDB[:, :], in_=cf_h[:, 128:256], inc=(dWG, 16))
    GE.dma_start(out=IDN[:, :], in_=cf_h[:, 128:256], inc=(dWG, 16))
    GE.dma_start(out=ONEC[:, :], in_=cf_h[:, 256:257], allow_slow_non_contiguous=True, inc=(dWG, 16))
    GE.dma_start(out=ROWS[:, 0, :], in_=cf_h[0:1, 257:385], inc=(dWG, 16))
    GE.dma_start(out=ROWS[:, 1, :], in_=cf_h[0:1, 385:513], inc=(dWG, 16))
    GE.dma_start(out=ROWS[:, 2, :], in_=cf_h[0:1, 513:641], inc=(dWG, 16))
    GE.dma_start(out=UPS[:, :], in_=upsc_h[:, :], inc=(dWG, 16))
    GE.dma_start(out=DNS[:, :], in_=dnsc_h[:, :], inc=(dWG, 16))
    GE.dma_start(out=OSS[:, :], in_=osc_h[:, :], inc=(dWG, 16))
    GE.dma_start(out=MGC[:, :], in_=magic_h[:, :], inc=(dWG, 16))
    CONST_LD = dWG.v
    VE.memset(ZRO[:, :], 0)
    VE.memset(EPSC[:, :], EPS)
    VE.memset(NRM[:, 0, :], 0.0)
    VE.memset(NRM[:, 1, :], 0.0, inc=(vS, 1))

    TE.wait(dWG, CONST_LD)
    pwait(P[6])
    TE.matmul(P[6][:, 0:1], ROWS[:, 2, :], OSS[:, :], start=True, stop=True, inc=(tS, 1))
    VE.wait(tS)
    VE.tensor_copy(TD[:, 0:1], P[6][:, 0:1], inc=(vS, 1))
    pdone(P[6], vS)
    SE.wait(vS)
    SE.activation(CMB[:, :], DNS[:, :], AF.Copy, scale=TD[:, 0:1], inc=(cS, 1))
    CMB_DONE = cS.v

    def emit_ln(out_buf, osl, xready, outfree):
        VE.wait(*xready)
        SE.wait(*xready)
        lnmm = {}
        for dc in range(DC):
            b = dc % 2
            if dc >= 2:
                VE.wait(tS, lnmm[dc - 2])
                SE.wait(tS, lnmm[dc - 2])
            VE.tensor_copy(XBF[:, b, :], XB[:, dc, :], inc=(vS, 1))
            SE.activation(XSQ[:, b, :], XB[:, dc, :], AF.Square, inc=(cS, 1))
            TE.wait(vS)
            TE.wait(cS)
            if dc == 0:
                pwait(P[6])
                pwait(P[7])
            TE.matmul(P[6][0:1, :], ONEC[:, :], XBF[:, b, :], start=(dc == 0), stop=(dc == DC - 1))
            TE.matmul(P[7][0:1, :], ONEC[:, :], XSQ[:, b, :], start=(dc == 0), stop=(dc == DC - 1),
                      inc=(tS, 1))
            lnmm[dc] = tS.v
        VE.wait(tS)
        VE.tensor_scalar_mul(STAT[:, :, 0], P[6][0:1, :], 1.0 / D, inc=(vS, 1))
        VE.tensor_scalar_mul(STAT[:, :, 1], P[7][0:1, :], 1.0 / D, inc=(vS, 1))
        pdone(P[6], vS)
        pdone(P[7], vS)
        VE.wait(vS)
        VE.tensor_tensor(STAT[:, :, 2], STAT[:, :, 0], STAT[:, :, 0], ALU.mult, inc=(vS, 1))
        VE.wait(vS)
        VE.tensor_tensor(STAT[:, :, 1], STAT[:, :, 1], STAT[:, :, 2], ALU.subtract, inc=(vS, 1))
        SE.wait(vS)
        SE.activation(STAT[:, :, 1], STAT[:, :, 1], AF.Ln, bias=EPSC[0:1, :])
        SE.activation(STAT[:, :, 1], STAT[:, :, 1], AF.Exp, scale=-0.5, inc=(cS, 1))
        VE.wait(cS)
        VE.tensor_tensor(STAT[:, :, 2], STAT[:, :, 0], STAT[:, :, 1], ALU.mult, inc=(vS, 1))
        VE.wait(vS)
        TE.wait(vS)
        pwait(P[6])
        TE.matmul(P[6][:, :], ROWS[:, 0, :], STAT[:, :, 1], start=True, stop=True)
        pwait(P[7])
        TE.matmul(P[7][:, :], ROWS[:, 0, :], STAT[:, :, 2], start=True, stop=True, inc=(tS, 1))
        VE.wait(tS)
        VE.tensor_copy(RSB[:, :], P[6][:, :])
        VE.tensor_copy(MRSB[:, :], P[7][:, :], inc=(vS, 1))
        pdone(P[6], vS)
        pdone(P[7], vS)
        GE.wait(vS)
        if outfree is not None:
            GE.wait(*outfree)
        pol = {}
        for dc in range(DC):
            if dc > 1:
                VE.wait(pS, pol[dc - 2])
            VE.tensor_tensor(TF[:, dc % 2, :], XB[:, dc, :], RSB[:, :], ALU.mult, inc=(vS, 1))
            GE.wait(vS)
            GE.tensor_tensor(out_buf[:, dc, osl], TF[:, dc % 2, :], MRSB[:, :], ALU.subtract,
                             inc=(pS, 1))
            pol[dc] = pS.v
        return pS.v

    def hidsl(hc):
        if hc < 32:
            return KRB[:, hc // 2, (hc % 2) * TB:(hc % 2) * TB + TB]
        h2 = hc - 32
        return VTB[:, h2 // 4, (h2 % 4) * TB:(h2 % 4) * TB + TB]

    def emit_ffn(lnsl, ln_ready, wmode, hid_free, scr_gate=None, xdst=None):
        slab = {}
        amm = {}
        relu_done = {}
        sgn = {}
        scrw = {}
        NB = 4
        for step in range(HC + 1):
            if step < HC:
                hc = step
                b = hc % NB
                if wmode == "fp32":
                    if hc >= 1:
                        GE.wait(cS, sgn[hc - 1])
                    GE.dma_start(out=WF[:, :], in_=up_h[hc, :, :, :].rearrange("p a b -> p (a b)"), inc=(dWG, 16))
                    SE.wait(dWG)
                    SE.wait(*wfb_busy[b])
                    if hc >= NB:
                        SE.wait(dWG, scrw[hc - NB])
                    SE.activation(WFB[:, b, :], WF[:, :], AF.Sign, inc=(cS, 1))
                    sgn[hc] = cS.v
                    GE.wait(cS, sgn[hc])
                    GE.dma_start(out=upsgn_h[hc, :, :, :],
                                 in_=WFB[:, b, :].rearrange("p (a c) -> p a c", a=DC), inc=(dWG, 16))
                    scrw[hc] = dWG.v
                    slab[hc] = (cS, sgn[hc])
                else:
                    if hc == 0 and scr_gate is not None:
                        SY.wait(*scr_gate)
                    SY.wait(*wfb_busy[b])
                    SY.dma_start(out=WFB[:, b, :].rearrange("p (a c) -> p a c", a=DC),
                                 in_=upsgn_h[hc, :, :, :], inc=(dWS, 16))
                    slab[hc] = (dWS, dWS.v)
            if step >= 1:
                h = step - 1
                b = h % NB
                pb = P[h % 4]
                ssem, scnt = slab[h]
                TE.wait(ssem, scnt)
                if h == 0:
                    TE.wait(pS, ln_ready)
                    if hid_free is not None:
                        SE.wait(*hid_free)
                if h >= 4:
                    TE.wait(cS, relu_done[h - 4])
                else:
                    pwait(pb)
                for dc in range(DC):
                    TE.matmul(pb[:, :], WFB[:, b, dc * 128:dc * 128 + 128], LNB[:, dc, lnsl],
                              start=(dc == 0), stop=(dc == DC - 1),
                              inc=(tS, 1) if dc == DC - 1 else None)
                amm[h] = tS.v
                wfb_busy[b] = (tS, amm[h])
                SE.wait(tS, amm[h])
                SE.activation(hidsl(h), pb[:, :], AF.Relu, scale=UPS[:, h:h + 1], inc=(cS, 1))
                relu_done[h] = cS.v
                pdone(pb, cS)
        a_done = cS.v
        bmm = {}
        bsgn = {}
        bscrw = {}
        last_res = 0
        NB = 4
        for dcg in range(8):
            pwait(P[2])
            pwait(P[3])
            for hg8 in range(8):
                gi = dcg * 8 + hg8
                b = gi % NB
                if wmode == "fp32":
                    if gi >= 1:
                        GE.wait(cS, bsgn[gi - 1])
                    GE.dma_start(out=WF[:, :], in_=dn_h[dcg, :, hg8 * 8:hg8 * 8 + 8, :].rearrange("p a b -> p (a b)"),
                                 inc=(dWG, 16))
                    SE.wait(dWG)
                    SE.wait(*wfb_busy[b])
                    if gi >= NB:
                        SE.wait(dWG, bscrw[gi - NB])
                    SE.activation(WFB[:, b, :], WF[:, :], AF.Sign, inc=(cS, 1))
                    bsgn[gi] = cS.v
                    GE.wait(cS, bsgn[gi])
                    GE.dma_start(out=dnsgn_h[dcg, :, hg8 * 8:hg8 * 8 + 8, :],
                                 in_=WFB[:, b, :].rearrange("p (a c) -> p a c", a=8), inc=(dWG, 16))
                    bscrw[gi] = dWG.v
                    bload = (cS, bsgn[gi])
                else:
                    if gi == 0 and scr_gate is not None:
                        SY.wait(*scr_gate)
                    SY.wait(*wfb_busy[b])
                    SY.dma_start(out=WFB[:, b, :].rearrange("p (a c) -> p a c", a=8),
                                 in_=dnsgn_h[dcg, :, hg8 * 8:hg8 * 8 + 8, :], inc=(dWS, 16))
                    bload = (dWS, dWS.v)
                TE.wait(bload[0], bload[1])
                if gi == 0:
                    TE.wait(cS, a_done)
                for hloc in range(8):
                    hc = hg8 * 8 + hloc
                    st = (hg8 == 0 and hloc == 0)
                    sp = (hg8 == 7 and hloc == 7)
                    TE.matmul(P[2][:, :], WFB[:, b, hloc * 256:hloc * 256 + 128], hidsl(hc),
                              start=st, stop=sp)
                    TE.matmul(P[3][:, :], WFB[:, b, hloc * 256 + 128:hloc * 256 + 256], hidsl(hc),
                              start=st, stop=sp, inc=(tS, 1) if hloc == 7 else None)
                bmm[gi] = tS.v
                wfb_busy[b] = (tS, bmm[gi])
            SE.wait(tS, bmm[dcg * 8 + 7])
            SE.wait(cS, CMB_DONE)
            if dcg >= 1:
                SE.wait(vS, last_res)
            SE.activation(TD[:, :], P[2][:, :], AF.Copy, scale=CMB[:, 2 * dcg:2 * dcg + 1],
                          inc=(cS, 1))
            VE.wait(cS)
            VE.tensor_tensor(XB[:, 2 * dcg, :], XB[:, 2 * dcg, :], TD[:, :], ALU.add, inc=(vS, 1))
            SE.wait(vS)
            SE.activation(TD[:, :], P[3][:, :], AF.Copy, scale=CMB[:, 2 * dcg + 1:2 * dcg + 2],
                          inc=(cS, 1))
            pdone(P[2], cS)
            pdone(P[3], cS)
            VE.wait(cS)
            VE.tensor_tensor(XB[:, 2 * dcg + 1, :], XB[:, 2 * dcg + 1, :], TD[:, :], ALU.add,
                             inc=(vS, 1))
            last_res = vS.v
            if xdst is not None:
                SY.wait(vS, last_res)
                SY.dma_start(out=xdst[:, 2 * dcg:2 * dcg + 2, :],
                             in_=XB[:, 2 * dcg:2 * dcg + 2, :], inc=(dIO, 16))
        return vS.v

    def emit_proj(out_buf, osl, lnsl, ln_ready, wsrc, rope, qoff, cs_ready, scr_gate=None):
        wload = {}
        pmm = {}
        emarks = {}
        ropemm = {}
        NB = 4
        for step in range(DC + 1):
            if step < DC:
                mc = step
                b = mc % NB
                if wsrc[0] == "cast":
                    GE.wait(*wfb_busy[b])
                    GE.dma_start(out=WFB[:, b, :].rearrange("p (a c) -> p a c", a=DC),
                                 in_=wsrc[1][mc, :, :, :], inc=(dWG, 16))
                    if wsrc[3]:
                        GE.wait(dWG)
                        GE.dma_start(out=wsrc[2][mc, :, :, :],
                                     in_=WFB[:, b, :].rearrange("p (a c) -> p a c", a=DC),
                                     inc=(dWG, 16))
                    wload[mc] = (dWG, dWG.v)
                else:
                    if mc == 0 and scr_gate is not None:
                        SY.wait(*scr_gate)
                    SY.wait(*wfb_busy[b])
                    SY.dma_start(out=WFB[:, b, :].rearrange("p (a c) -> p a c", a=DC),
                                 in_=wsrc[1][mc, :, :, :], inc=(dWS, 16))
                    wload[mc] = (dWS, dWS.v)
            if step >= 1:
                m = step - 1
                b = m % NB
                pb = P[m % 4]
                TE.wait(wload[m][0], wload[m][1])
                if m == 0:
                    TE.wait(pS, ln_ready)
                    if cs_ready is not None:
                        VE.wait(*cs_ready)
                        GE.wait(*cs_ready)
                if m >= 4:
                    TE.wait(cS, emarks[m - 4])
                else:
                    pwait(pb)
                for dc in range(DC):
                    TE.matmul(pb[:, :], WFB[:, b, dc * 128:dc * 128 + 128], LNB[:, dc, lnsl],
                              start=(dc == 0), stop=(dc == DC - 1),
                              inc=(tS, 1) if dc == DC - 1 else None)
                pmm[m] = tS.v
                wfb_busy[b] = (tS, pmm[m])
                if rope is None:
                    SE.wait(tS, pmm[m])
                    SE.activation(out_buf[:, m, osl], pb[:, :], AF.Copy, inc=(cS, 1))
                    emarks[m] = cS.v
                    pdone(pb, cS)
                else:
                    SE.wait(tS, pmm[m])
                    if m >= 1:
                        SE.wait(tS, ropemm[m - 1])
                    SE.activation(KPRE, pb[:, :], AF.Copy, inc=(cS, 1))
                    emarks[m] = cS.v
                    pdone(pb, cS)
                    VE.wait(cS, emarks[m])
                    GE.wait(cS, emarks[m])
                    GE.tensor_tensor(T1, KPRE, CSB[:, 1, qoff:qoff + TB], ALU.mult,
                                     inc=(pS, 1))
                    VE.tensor_tensor(T2, KPRE, CSB[:, 0, qoff:qoff + TB], ALU.mult,
                                     inc=(vS, 1))
                    TE.wait(pS)
                    TE.wait(vS)
                    pwait(P[4])
                    TE.matmul(P[4][:, :], PRM[:, :], T1, start=True, stop=False)
                    TE.matmul(P[4][:, :], IDB[:, :], T2, start=False, stop=True, inc=(tS, 1))
                    ropemm[m] = tS.v
                    SE.wait(tS, ropemm[m])
                    SE.activation(out_buf[:, m, osl], P[4][:, :], AF.Copy, inc=(cS, 1))
                    pdone(P[4], cS)
        return cS.v, tS.v

    def emit_vproj(lnsl, ln_ready, wsrc, tb, scr_gate=None):
        wload = {}
        pmm = {}
        evd = {}
        trd = {}
        NB = 4
        for step in range(DC + 1):
            if step < DC:
                mc = step
                b = mc % NB
                if wsrc[0] == "cast":
                    GE.wait(*wfb_busy[b])
                    GE.dma_start(out=WFB[:, b, :].rearrange("p (a c) -> p a c", a=DC),
                                 in_=wsrc[1][mc, :, :, :], inc=(dWG, 16))
                    if wsrc[3]:
                        GE.wait(dWG)
                        GE.dma_start(out=wsrc[2][mc, :, :, :],
                                     in_=WFB[:, b, :].rearrange("p (a c) -> p a c", a=DC),
                                     inc=(dWG, 16))
                    wload[mc] = (dWG, dWG.v)
                else:
                    if mc == 0 and scr_gate is not None:
                        SY.wait(*scr_gate)
                    SY.wait(*wfb_busy[b])
                    SY.dma_start(out=WFB[:, b, :].rearrange("p (a c) -> p a c", a=DC),
                                 in_=wsrc[1][mc, :, :, :], inc=(dWS, 16))
                    wload[mc] = (dWS, dWS.v)
            if step >= 1:
                m = step - 1
                b = m % NB
                pb = P[m % 4]
                TE.wait(wload[m][0], wload[m][1])
                if m == 0:
                    TE.wait(pS, ln_ready)
                if m >= 4:
                    TE.wait(vS, evd[m - 4])
                else:
                    pwait(pb)
                for dc in range(DC):
                    TE.matmul(pb[:, :], WFB[:, b, dc * 128:dc * 128 + 128], LNB[:, dc, lnsl],
                              start=(dc == 0), stop=(dc == DC - 1),
                              inc=(tS, 1) if dc == DC - 1 else None)
                pmm[m] = tS.v
                wfb_busy[b] = (tS, pmm[m])
                VE.wait(tS, pmm[m])
                if m >= 1:
                    VE.wait(tS, trd[m - 1])
                VE.tensor_copy(TF[:, m % 2, :], pb[:, :], inc=(vS, 1))
                evd[m] = vS.v
                pdone(pb, vS)
                TE.wait(vS, evd[m])
                for i in range(4):
                    pt = P[4 + (i % 2)]
                    pwait(pt)
                    TE.transpose(pt[:, 0:128], TF[:, m % 2, i * 128:i * 128 + 128], IDN[:, :],
                                 inc=(tS, 1))
                    SE.wait(tS)
                    SE.activation(VTB[:, tb * 4 + i, m * 128:m * 128 + 128], pt[:, 0:128],
                                  AF.Copy, inc=(cS, 1))
                    pdone(pt, cS)
                trd[m] = tS.v
        return cS.v

    def emit_norms(buf, ncols, col, ready):
        SE.wait(*ready)
        for h in range(NH):
            SE.activation(KSQ[:, 0:ncols], buf[:, h, 0:ncols], AF.Square, inc=(cS, 1))
            TE.wait(cS)
            pwait(P[6])
            TE.matmul(P[6][0:1, :], ONEC[:, :], KSQ[:, 0:TB], start=True, stop=True)
            pwait(P[7])
            TE.matmul(P[7][0:1, :], ONEC[:, :], KSQ[:, TB:2 * TB], start=True, stop=True,
                      inc=(tS, 1))
            VE.wait(tS)
            VE.reduce_max(KNT[:, 0:1], P[6][0:1, :], axis=AX.X, inc=(vS, 1))
            VE.reduce_max(KNT[:, 1:2], P[7][0:1, :], axis=AX.X, inc=(vS, 1))
            pdone(P[6], vS)
            pdone(P[7], vS)
            VE.wait(vS)
            VE.tensor_tensor(KNT[:, 0:1], KNT[:, 0:1], KNT[:, 1:2], ALU.max, inc=(vS, 1))
            VE.wait(vS)
            VE.tensor_tensor(NRM[:, col, h:h + 1], NRM[:, col, h:h + 1], KNT[:, 0:1], ALU.max,
                             inc=(vS, 1))
        return vS.v

    def arm_info(arm):
        log = physical_to_logical_core_id(int(arm) % 128, arm.node_id_4,
                                          container_num_cores=8) % 8
        return log, log % 2

    regs = {}

    xb_free_v = 0
    xb_free_io = 0
    lnb_free = {0: (tS, 0), 1: (tS, 0)}
    hid_free = None
    ffn_scr = None
    k_scr = None
    v_scr = None
    TBSL = [slice(0, TB), slice(TB, 2 * TB)]

    for slot in range(COV):
        SY.wait(pS)
        SY.wait(vS)
        SY.dma_start(out=CSB[:, 0, :], in_=cscov_h[slot, 0, :, :], inc=(dIO, 16))
        SY.dma_start(out=CSB[:, 1, :], in_=cscov_h[slot, 1, :, :], inc=(dIO, 16))
        cs_ready = (dIO, dIO.v)
        for tb in range(2):
            wmode = "fp32" if (slot == 0 and tb == 0) else "bf16"
            SY.wait(vS, xb_free_v)
            SY.wait(dIO, xb_free_io)
            SY.dma_start(out=XB[:, :, :], in_=xcov_h[slot, :, :, TBSL[tb]], inc=(dIO, 16))
            xready = (dIO, dIO.v)
            lnr = emit_ln(LNB, TBSL[tb], xready, lnb_free[tb])
            res = emit_ffn(TBSL[tb], lnr, wmode, hid_free, scr_gate=ffn_scr,
                           xdst=xmid_h[slot, :, :, TBSL[tb]])
            if slot == 0 and tb == 0:
                ffn_scr = (dWG, dWG.v)
            hid_free = None
            lnb_free[tb] = (tS, tS.v)
            xb_free_v = res
            xb_free_io = dIO.v
        for tb in range(2):
            SY.wait(vS, xb_free_v)
            SY.wait(dIO, xb_free_io)
            SY.dma_start(out=XB[:, :, :], in_=xmid_h[slot, :, :, TBSL[tb]], inc=(dIO, 16))
            xready = (dIO, dIO.v)
            lnr = emit_ln(LNB, TBSL[tb], xready, lnb_free[tb])
            ksrc = ("cast", wk_h, wkbf_h, True) if (slot == 0 and tb == 0) else ("scr", wkbf_h)
            kdone, kmm = emit_proj(KRB, TBSL[tb], TBSL[tb], lnr, ksrc, "rope", tb * TB,
                                   cs_ready, scr_gate=k_scr)
            if slot == 0 and tb == 0:
                k_scr = (dWG, dWG.v)
            vsrc = ("cast", wv_h, wvbf_h, True) if (slot == 0 and tb == 0) else ("scr", wvbf_h)
            vdone = emit_vproj(TBSL[tb], lnr, vsrc, tb, scr_gate=v_scr)
            if slot == 0 and tb == 0:
                v_scr = (dWG, dWG.v)
            lnb_free[tb] = (tS, tS.v)
            xb_free_v = vS.v
        emit_norms(KRB, TPC, 0, (cS, cS.v))
        SY.wait(cS)
        SY.wait(vS)
        kvw = []
        for hq in range(4):
            dSH.v += 16
        for tt in range(8):
            dSH.v += 16
        pre = dSH.v

        def mk_kv_writer(slotc, target):
            def go(sy):
                for arm in nc.switch_core_id([nc.sync], n=128):
                    log, par = arm_info(arm)
                    g = 2 * slotc + par
                    for hq in range(4):
                        sy.dma_start(
                            out=shk_h[:, hq * 4:hq * 4 + 4, g * 1024:(g + 1) * 1024],
                            in_=KRB[:, hq * 4:hq * 4 + 4, :],
                        ).then_inc(target, 16)
                    for tt in range(8):
                        sy.dma_start(
                            out=shv_h[g * 1024 + tt * 128:g * 1024 + (tt + 1) * 128, :],
                            in_=VTB[:, tt, :],
                        ).then_inc(target, 16)
            return go

        SY.raw(mk_kv_writer(slot, dSH.h))
        hid_free = (dSH, dSH.v)

    knorm_ready = vS.v
    SY.wait(vS, knorm_ready)
    all_sh = dSH.v
    dSH.v += 16
    dSH.v += 16
    dSH.v += 16
    BAR_END = dSH.v

    def barrier(sy):
        r_mg = sy.register("r_mg").__enter__()
        r_pf = sy.register("r_pf").__enter__()
        r_nd = sy.register("r_nd").__enter__()
        regs["mg"], regs["pf"], regs["nd"] = r_mg, r_pf, r_nd
        for arm in nc.switch_core_id([nc.sync], n=128):
            log, par = arm_info(arm)
            other = 1 - par
            sy.dma_start(out=shkn_h[par:par + 1, :], in_=NRM[:, 0, :]).then_inc(dSH.h, 16)
            sy.wait_ge(dSH.h, all_sh + 16)
            sy.dma_start(out=shfl_h[par:par + 1, :], in_=MGC[:, :]).then_inc(dSH.h, 16)
            sy.wait_ge(dSH.h, all_sh + 32)
            sy.load(r_mg, MGC[0:1, 0:1])
            sy.reg_mov(r_nd, 1)
            with sy.While(r_nd):
                sy.load(r_pf, shfl_h[other:other + 1, 0:1])
                sy.reg_alu(r_nd, r_pf, r_mg, ALU.not_equal)
            sy.dma_start(out=PKN[:, :], in_=shkn_h[other:other + 1, :]).then_inc(dSH.h, 16)

    SY.raw(barrier)
    pkn_ready = BAR_END

    SY.wait(pS)
    SY.wait(vS)
    SY.dma_start(out=CSB[:, 0, :], in_=csown_h[0, :, :], inc=(dIO, 16))
    SY.dma_start(out=CSB[:, 1, :], in_=csown_h[1, :, :], inc=(dIO, 16))
    cs_own = (dIO, dIO.v)

    qdone = 0
    for tb in range(2):
        SY.wait(vS, xb_free_v)
        SY.wait(dIO, xb_free_io)
        dIO.v += 16

        def mk_xmid_rd(tbc, target, cnt):
            def go(sy):
                for arm in nc.switch_core_id([nc.sync], n=128):
                    log, par = arm_info(arm)
                    myslot = log // 2
                    sy.dma_start(out=XB[:, :, :],
                                 in_=xmid_h[myslot, :, :, TBSL[tbc]]).then_inc(target, 16)
                sy.wait_ge(target, cnt)
            return go

        SY.raw(mk_xmid_rd(tb, dIO.h, dIO.v))
        xready = (dIO, dIO.v)
        lnr = emit_ln(LNB, TBSL[tb], xready, lnb_free[tb])
        qsrc = ("cast", wq_h, None, False)
        qdone, _ = emit_proj(KRB, TBSL[tb], TBSL[tb], lnr, qsrc, "rope", tb * TB, cs_own)
        lnb_free[tb] = (tS, tS.v)
        xb_free_v = vS.v
    emit_norms(KRB, TPC, 1, (cS, qdone))

    VE.wait(dSH, pkn_ready)
    VE.wait(vS)
    VE.tensor_tensor(NRM[:, 2, :], NRM[:, 0, :], PKN[:, :], ALU.max, inc=(vS, 1))
    VE.wait(vS)
    VE.tensor_tensor(NRM[:, 2, :], NRM[:, 2, :], NRM[:, 1, :], ALU.mult, inc=(vS, 1))
    VE.wait(vS)
    VE.tensor_scalar_max(NRM[:, 2, :], NRM[:, 2, :], 1e-6, inc=(vS, 1))
    SE.wait(vS)
    SE.activation(NRM[:, 3, :], NRM[:, 2, :], AF.Sqrt, scale=1.0 / 128.0, inc=(cS, 1))
    VE.wait(cS)
    VE.tensor_scalar(NRM[:, 3, :], NRM[:, 3, :], 1.01, 0.5, ALU.mult, op1=ALU.add, inc=(vS, 1))
    TE.wait(vS)
    pwait(P[6])
    TE.matmul(P[6][:, 0:16], ROWS[:, 1, :], NRM[:, 3, :], start=True, stop=True, inc=(tS, 1))
    VE.wait(tS)
    VE.tensor_copy(NEGM[:, :], P[6][:, 0:16], inc=(vS, 1))
    pdone(P[6], vS)
    negm_ready = vS.v

    q_ready = qdone

    exp_done = {}
    pv_mm = {}
    acc_done = {}
    scr_free = {}
    atn_marks = []
    for qh in range(2):
        qsl = slice(qh * TB, qh * TB + TB)
        for (h0, h1) in HGROUPS:
            g = h1 - h0
            kready = {}
            vready = {}
            kcons = {}
            pvdone = {}

            def kload(j):
                if j >= 2:
                    SY.wait(tS, kcons[j - 2])
                for hi in range(g):
                    SY.dma_start(out=KSL[:, j % 2, hi, :],
                                 in_=shk_h[:, h0 + hi, j * 256:(j + 1) * 256],
                                 inc=(dKV, 16))
                kready[j] = dKV.v

            def vload(kc):
                if kc >= 2:
                    SY.wait(tS, pvdone[kc - 2])
                SY.dma_start(out=VSL[:, kc % 2, 0:g * 128],
                             in_=shv_h[kc * 128:(kc + 1) * 128, h0 * 128:h1 * 128],
                             inc=(dKV, 16))
                vready[kc] = dKV.v

            SY.wait(dSH, pkn_ready)
            kload(0)
            kload(1)
            vload(0)
            vload(1)
            pit = 0
            pending = []

            def flush_pv(keep):
                while len(pending) > keep:
                    hi2, j2, pit2 = pending.pop(0)
                    TE.wait(cS, exp_done[pit2])
                    a2 = 2 * (pit2 % 2)
                    for tt2 in range(2):
                        kc2 = 2 * j2 + tt2
                        TE.wait(dKV, vready[kc2])
                        if kc2 == 0:
                            pwait(P[hi2])
                        TE.matmul(P[hi2][:, :], VSL[:, kc2 % 2, hi2 * 128:(hi2 + 1) * 128],
                                  EXB[:, a2 + tt2, :], start=(kc2 == 0), stop=(kc2 == 63),
                                  inc=(tS, 1))
                        pv_mm[pit2] = tS.v
                        if hi2 == g - 1:
                            pvdone[kc2] = tS.v
                            if kc2 + 2 < 64:
                                vload(kc2 + 2)

            for j in range(32):
                TE.wait(dKV, kready[j])
                for hi in range(g):
                    h = h0 + hi
                    sb2 = PS45 if pit % 2 == 0 else PS67
                    if pit == 0:
                        pwait(P[4])
                        pwait(P[5])
                        TE.wait(cS, q_ready)
                        TE.wait(vS, negm_ready)
                    elif pit == 1:
                        pwait(P[6])
                        pwait(P[7])
                    else:
                        TE.wait(cS, exp_done[pit - 2])
                    TE.matmul(sb2[:, 0:512], KSL[:, j % 2, hi, 0:128],
                              KRB[:, h, qsl], start=True, stop=True)
                    TE.matmul(sb2[:, 512:1024], KSL[:, j % 2, hi, 128:256],
                              KRB[:, h, qsl], start=True, stop=True, inc=(tS, 1))
                    smm = tS.v
                    flush_pv(1)
                    SE.wait(tS, smm)
                    if pit >= 4:
                        SE.wait(vS, acc_done[pit - 4])
                    a = 2 * (pit % 2)
                    SE.activation(EXB[:, a:a + 2, :].rearrange("p x t -> p (x t)"), sb2[:, :],
                                  AF.Exp, bias=NEGM[:, h:h + 1], scale=INVSQHD, inc=(cS, 1))
                    exp_done[pit] = cS.v
                    VE.wait(cS, exp_done[pit])
                    if j == 0:
                        VE.tensor_copy(ACC[:, hi, :], EXB[:, a, :])
                    else:
                        VE.tensor_tensor(ACC[:, hi, :], ACC[:, hi, :], EXB[:, a, :], ALU.add)
                    VE.tensor_tensor(ACC[:, hi, :], ACC[:, hi, :], EXB[:, a + 1, :], ALU.add,
                                     inc=(vS, 1))
                    acc_done[pit] = vS.v
                    pending.append((hi, j, pit))
                    pit += 1
                kcons[j] = tS.v
                if j + 2 < 32:
                    kload(j + 2)
            flush_pv(0)
            pdone(P[4], cS)
            pdone(P[5], cS)
            pdone(P[6], cS)
            pdone(P[7], cS)
            last_pv = tS.v
            for hi in range(g):
                h = h0 + hi
                VE.wait(tS, last_pv)
                VE.tensor_copy(T1, ACC[:, hi, :], inc=(vS, 1))
                TE.wait(vS)
                pwait(P[6])
                TE.matmul(P[6][0:1, :], ONEC[:, :], T1, start=True, stop=True, inc=(tS, 1))
                VE.wait(tS)
                VE.reciprocal(STAT[:, :, 0], P[6][0:1, :], inc=(vS, 1))
                pdone(P[6], vS)
                TE.wait(vS)
                pwait(P[7])
                TE.matmul(P[7][:, :], ROWS[:, 0, :], STAT[:, :, 0], start=True, stop=True,
                          inc=(tS, 1))
                VE.wait(tS)
                VE.tensor_copy(RSB[:, :], P[7][:, :], inc=(vS, 1))
                pdone(P[7], vS)
                VE.tensor_tensor(ATN[:, h * 1024 + qh * TB:h * 1024 + qh * TB + TB],
                                 P[hi][:, :], RSB[:, :], ALU.mult, inc=(vS, 1))
                pdone(P[hi], vS)
            atn_marks.append(vS.v)
    atn_ready = vS.v

    for tb in range(2):
        SY.wait(vS, xb_free_v)
        SY.wait(dIO, xb_free_io)
        dIO.v += 16
        SY.raw(mk_xmid_rd(tb, dIO.h, dIO.v))
        xready = (dIO, dIO.v)
        owload = {}
        omm = {}
        oev = {}
        NB = 4
        for step in range(DC + 1):
            if step < DC:
                mc = step
                b = mc % NB
                GE.wait(*wfb_busy[b])
                GE.dma_start(out=WFB[:, b, :].rearrange("p (a c) -> p a c", a=DC),
                             in_=wo_h[mc, :, :, :], inc=(dWG, 16))
                owload[mc] = (dWG, dWG.v)
            if step >= 1:
                m = step - 1
                b = m % NB
                pb = P[m % 4]
                TE.wait(owload[m][0], owload[m][1])
                if m == 0:
                    TE.wait(vS, atn_ready)
                if m >= 4:
                    TE.wait(vS, oev[m - 4])
                else:
                    pwait(pb)
                for dc in range(DC):
                    TE.matmul(pb[:, :], WFB[:, b, dc * 128:dc * 128 + 128],
                              ATN[:, dc * 1024 + tb * TB:dc * 1024 + tb * TB + TB],
                              start=(dc == 0), stop=(dc == DC - 1),
                              inc=(tS, 1) if dc == DC - 1 else None)
                omm[m] = tS.v
                wfb_busy[b] = (tS, omm[m])
                VE.wait(tS, omm[m])
                if m == 0:
                    VE.wait(*xready)
                VE.tensor_tensor(XB[:, m, :], XB[:, m, :], pb[:, :], ALU.add, inc=(vS, 1))
                oev[m] = vS.v
                pdone(pb, vS)
        SY.wait(vS, oev[DC - 1])
        SY.dma_start(out=x1_h[:, :, TBSL[tb]], in_=XB[:, :, :], inc=(dIO, 16))
        xb_free_v = vS.v
        xb_free_io = dIO.v
    o_done = tS.v

    hid_free = (tS, o_done)
    for tb in range(2):
        SY.wait(vS, xb_free_v)
        SY.wait(dIO, xb_free_io)
        SY.dma_start(out=XB[:, :, :], in_=x1_h[:, :, TBSL[tb]], inc=(dIO, 16))
        xready = (dIO, dIO.v)
        lnr = emit_ln(LNB, TBSL[tb], xready, lnb_free[tb])
        res = emit_ffn(TBSL[tb], lnr, "bf16", hid_free, scr_gate=ffn_scr)
        hid_free = None
        lnb_free[tb] = (tS, tS.v)
        SY.wait(vS, res)
        SY.dma_start(out=out_h[:, :, TBSL[tb]], in_=XB[:, :, :], inc=(dIO, 16))
        xb_free_v = res
        xb_free_io = dIO.v

    dIO.v += 16

    def resetter(sy):
        for arm in nc.switch_core_id([nc.sync], n=128):
            log, par = arm_info(arm)
            sy.dma_start(out=shfl_h[par:par + 1, :], in_=ZRO[:, :]).then_inc(dIO.h, 16)

    SY.raw(resetter)
    SY.wait(dIO)
    SY.wait(dSH)
    SY.wait(dKV)

    with nc.Block() as block:
        def replay(engine_name):
            def body(eng):
                for (en, op, a, k, inc) in script:
                    if en != engine_name:
                        continue
                    if op is None:
                        a[0](eng)
                        continue
                    inst = getattr(eng, op)(*a, **k)
                    if inc is not None:
                        inst.then_inc(inc[0], inc[1])
            return body

        block.tensor(replay("tensor"))
        block.vector(replay("vector"))
        block.scalar(replay("scalar"))
        block.gpsimd(replay("gpsimd"))
        block.sync(replay("sync"))

    for t in reversed(ctx):
        t.__exit__(None, None, None)
    return nc


_CACHE = {}


def _stage_weights(L0_attn_wq, L0_attn_wk, L0_attn_wv, L0_attn_wo,
                   L0_ffn_up_weight, L0_ffn_down_weight,
                   L0_ffn_up_scale, L0_ffn_down_scale, L0_ffn_output_scale):
    st = {}

    def wstage(w):
        return np.ascontiguousarray(w.reshape(DC, 128, DC, 128).transpose(0, 3, 2, 1))

    st["wqt"] = wstage(L0_attn_wq)
    st["wkt"] = wstage(L0_attn_wk)
    st["wvt"] = wstage(L0_attn_wv)
    st["wot"] = wstage(L0_attn_wo)
    st["upt"] = np.ascontiguousarray(
        L0_ffn_up_weight.reshape(HC, 128, DC, 128).transpose(0, 3, 2, 1))
    dw = L0_ffn_down_weight.reshape(8, 2, 128, HC, 128)
    st["dnt"] = np.ascontiguousarray(
        dw.transpose(0, 4, 3, 1, 2).reshape(8, 128, HC, 256))
    st["upsc"] = np.ascontiguousarray(L0_ffn_up_scale.reshape(HC, 128).T)
    st["dnsc"] = np.ascontiguousarray(L0_ffn_down_scale.reshape(DC, 128).T)
    st["osc"] = np.ascontiguousarray(L0_ffn_output_scale.reshape(1, 1))
    return st


def _consts():
    cf = np.zeros((128, 648), np.float32)
    prm = np.zeros((128, 128), np.float32)
    for i in range(64):
        prm[2 * i + 1, 2 * i] = 1.0
        prm[2 * i, 2 * i + 1] = 1.0
    cf[:, 0:128] = prm
    cf[:, 128:256] = np.eye(128, dtype=np.float32)
    cf[:, 256] = 1.0
    cf[0, 257:385] = 1.0
    cf[0, 385:513] = -1.0
    cf[0, 513:641] = 0.5
    return cf


def _cs_tables():
    import ml_dtypes
    hd = 128
    scale = np.arange(0, hd, 2, dtype=np.float64) / hd
    inv = 1.0 / 10000.0 ** scale
    t = np.arange(SEQ, dtype=np.float64)
    ang = np.outer(inv, t)
    C = np.empty((128, SEQ), np.float64)
    S2 = np.empty((128, SEQ), np.float64)
    C[0::2] = np.cos(ang)
    C[1::2] = np.cos(ang)
    S2[0::2] = np.sin(ang)
    S2[1::2] = -np.sin(ang)
    return (C.astype(ml_dtypes.bfloat16), S2.astype(ml_dtypes.bfloat16))


def kernel(L0_attn_x, L0_attn_wq, L0_attn_wk, L0_attn_wv, L0_attn_wo,
           L0_ffn_up_weight, L0_ffn_down_weight, L0_ffn_up_scale,
           L0_ffn_down_scale, L0_ffn_output_scale):
    import ml_dtypes
    if "nc" not in _CACHE:
        _CACHE["nc"] = build_program()
    nc = _CACHE["nc"]

    st = _stage_weights(np.asarray(L0_attn_wq, np.float32), np.asarray(L0_attn_wk, np.float32),
                        np.asarray(L0_attn_wv, np.float32), np.asarray(L0_attn_wo, np.float32),
                        np.asarray(L0_ffn_up_weight, np.float32),
                        np.asarray(L0_ffn_down_weight, np.float32),
                        np.asarray(L0_ffn_up_scale, np.float32),
                        np.asarray(L0_ffn_down_scale, np.float32),
                        np.asarray(L0_ffn_output_scale, np.float32))
    cf = _consts()
    C, S2 = _cs_tables()
    x = np.asarray(L0_attn_x, np.float32).reshape(SEQ, D)
    magic = np.random.randint(1, 2 ** 31 - 1, size=(1, NMAGIC)).astype(np.int32)

    def fm(chunk):
        blk = x[chunk * 1024:(chunk + 1) * 1024, :]
        return np.ascontiguousarray(blk.T.reshape(DC, 128, 1024).transpose(1, 0, 2))

    in_maps = []
    for c in range(8):
        par = c % 2
        chunks = [par, par + 2, par + 4, par + 6]
        xcov = np.stack([fm(g) for g in chunks], axis=0)
        cscov = np.stack(
            [np.stack([C[:, g * 1024:(g + 1) * 1024], S2[:, g * 1024:(g + 1) * 1024]], axis=0)
             for g in chunks], axis=0)
        csown = np.stack([C[:, c * 1024:(c + 1) * 1024], S2[:, c * 1024:(c + 1) * 1024]], axis=0)
        m = dict(st)
        m["xcov"] = xcov
        m["cscov"] = np.ascontiguousarray(cscov)
        m["csown"] = np.ascontiguousarray(csown)
        m["constf"] = cf
        m["magic"] = magic
        in_maps.append(m)

    res = run_bass_kernel_spmd(nc, in_maps, core_ids=list(range(8)))
    out = np.empty((SEQ, D), np.float32)
    for c in range(8):
        arr = res.results[c]["out"]
        out[c * 1024:(c + 1) * 1024, :] = arr.transpose(1, 0, 2).reshape(D, 1024).T
    return out.reshape(1, SEQ, D)


# version 07
# speedup vs baseline: 1.3639x; 1.0933x over previous
import numpy as np

import concourse.bass as bass
import concourse.mybir as mybir
from concourse.bass_utils import run_bass_kernel_spmd
from concourse.libnrt import physical_to_logical_core_id

F32 = mybir.dt.float32
BF16 = mybir.dt.bfloat16
I32 = mybir.dt.int32
AF = mybir.ActivationFunctionType
ALU = mybir.AluOpType
AX = mybir.AxisListType

D = 2048
HID = 8192
SEQ = 8192
NH = 16
TPC = 1024
COV = 4
TB = 512
DC = 16
HC = 64
NMAGIC = 4
INVSQHD = float(1.0 / np.sqrt(128.0))
EPS = 1e-5
HGROUPS = [(0, 4), (4, 8), (8, 12), (12, 16)]


class Sem:
    def __init__(self, handle):
        self.h = handle
        self.v = 0


class Rec:

    def __init__(self, script, name):
        self.__dict__["_s"] = script
        self.__dict__["_n"] = name
        self.__dict__["_seen"] = {}

    def __getattr__(self, op):
        def rec(*a, inc=None, **k):
            pair = None
            if inc is not None:
                sem, n = inc
                sem.v += n
                pair = (sem.h, n)
            self._s.append((self._n, op, a, k, pair))
        return rec

    def wait(self, sem, v=None):
        t = sem.v if v is None else v
        if t <= 0:
            return
        if self._seen.get(id(sem), -1) >= t:
            return
        self._seen[id(sem)] = t
        self._s.append((self._n, "wait_ge", (sem.h, t), {}, None))

    def raw(self, fn):
        self._s.append((self._n, None, (fn,), {}, None))


def build_program():
    nc = bass.Bass("TRN2", target_bir_lowering=False)

    xcov_h = nc.dram_tensor("xcov", [COV, 128, DC, TPC], F32, kind="ExternalInput").ap()
    wq_h = nc.dram_tensor("wqt", [DC, 128, DC, 128], F32, kind="ExternalInput").ap()
    wk_h = nc.dram_tensor("wkt", [DC, 128, DC, 128], F32, kind="ExternalInput").ap()
    wv_h = nc.dram_tensor("wvt", [DC, 128, DC, 128], F32, kind="ExternalInput").ap()
    wo_h = nc.dram_tensor("wot", [DC, 128, DC, 128], F32, kind="ExternalInput").ap()
    up_h = nc.dram_tensor("upt", [HC, 128, DC, 128], F32, kind="ExternalInput").ap()
    dn_h = nc.dram_tensor("dnt", [8, 128, HC, 256], F32, kind="ExternalInput").ap()
    upsc_h = nc.dram_tensor("upsc", [128, HC], F32, kind="ExternalInput").ap()
    dnsc_h = nc.dram_tensor("dnsc", [128, DC], F32, kind="ExternalInput").ap()
    osc_h = nc.dram_tensor("osc", [1, 1], F32, kind="ExternalInput").ap()
    cscov_h = nc.dram_tensor("cscov", [COV, 2, 128, TPC], BF16, kind="ExternalInput").ap()
    csown_h = nc.dram_tensor("csown", [2, 128, TPC], BF16, kind="ExternalInput").ap()
    cf_h = nc.dram_tensor("constf", [128, 648], F32, kind="ExternalInput").ap()
    magic_h = nc.dram_tensor("magic", [1, NMAGIC], I32, kind="ExternalInput").ap()
    out_h = nc.dram_tensor("out", [128, DC, TPC], F32, kind="ExternalOutput").ap()

    upsgn_h = nc.dram_tensor("upsgn", [HC, 128, DC, 128], BF16, kind="Internal").ap()
    dnsgn_h = nc.dram_tensor("dnsgn", [8, 128, HC, 256], BF16, kind="Internal").ap()
    wkbf_h = nc.dram_tensor("wkbf", [DC, 128, DC, 128], BF16, kind="Internal").ap()
    wvbf_h = nc.dram_tensor("wvbf", [DC, 128, DC, 128], BF16, kind="Internal").ap()
    xmid_h = nc.dram_tensor("xmidscr", [COV, 128, DC, TPC], F32, kind="Internal").ap()
    x1_h = nc.dram_tensor("x1scr", [128, DC, TPC], F32, kind="Internal").ap()

    shk_h = nc.dram_tensor("shk", [128, NH, SEQ], BF16, kind="Internal", addr_space="Shared").ap()
    shv_h = nc.dram_tensor("shv", [SEQ, D], BF16, kind="Internal", addr_space="Shared").ap()
    shkn_h = nc.dram_tensor("shkn", [2, 16], F32, kind="Internal", addr_space="Shared").ap()
    shfl_h = nc.dram_tensor("shfl", [2, NMAGIC], I32, kind="Internal", addr_space="Shared").ap()

    ctx = []

    def enter(t):
        ctx.append(t)
        return t.__enter__()

    sbuf = lambda n, sh, dt: enter(nc.sbuf_tensor(n, sh, dt))
    psum = lambda n: enter(nc.psum_tensor(n, [128, 512], F32))

    XB = sbuf("XB", [128, DC, TB], F32)
    LNB = sbuf("LNB", [128, DC, TPC], BF16)
    KRB = sbuf("KRB", [128, NH, TPC], BF16)
    VTB = sbuf("VTB", [128, 8, D], BF16)
    WF = sbuf("WF", [128, 2048], F32)
    WFB = sbuf("WFB", [128, 4, 2048], BF16)
    KSL = sbuf("KSL", [128, 2, 4, 256], BF16)
    VSL = sbuf("VSL", [128, 4, 512], BF16)
    ACC = sbuf("ACC", [128, 4, TB], F32)
    EXB = sbuf("EXB", [128, 4, TB], BF16)
    KSQ = EXB[:, 0:2, :].rearrange("p a b -> p (a b)")
    T1 = EXB[:, 0, :]
    RT = sbuf("RT", [128, 6, TB], BF16)
    KPRE2 = RT[:, 0:2, :]
    T1s = RT[:, 2:4, :]
    T2s = RT[:, 4:6, :]
    TF = sbuf("TF", [128, 2, TB], F32)
    TD = sbuf("TD", [128, TB], F32)
    RSB = sbuf("RSB", [128, TB], F32)
    MRSB = sbuf("MRSB", [128, TB], F32)
    CSB = sbuf("CSB", [128, 2, TPC], BF16)
    XBF = sbuf("XBF", [128, 2, TB], BF16)
    XSQ = sbuf("XSQ", [128, 2, TB], BF16)
    STAT = sbuf("STAT", [1, TB, 3], F32)
    PRM = sbuf("PRM", [128, 128], BF16)
    IDB = sbuf("IDB", [128, 128], BF16)
    IDN = sbuf("IDN", [128, 128], F32)
    ONEC = sbuf("ONEC", [128, 1], BF16)
    ROWS = sbuf("ROWS", [1, 3, 128], F32)
    UPS = sbuf("UPS", [128, HC], F32)
    DNS = sbuf("DNS", [128, DC], F32)
    CMB = sbuf("CMB", [128, DC], F32)
    OSS = sbuf("OSS", [1, 1], F32)
    NRM = sbuf("NRM", [1, 4, 16], F32)
    KNT = sbuf("KNT", [1, 2], F32)
    NEGM = sbuf("NEGM", [128, 16], F32)
    EPSC = sbuf("EPSC", [128, 1], F32)
    MGC = sbuf("MGC", [1, NMAGIC], I32)
    ZRO = sbuf("ZRO", [1, NMAGIC], I32)
    PKN = sbuf("PKN", [1, 16], F32)

    P03 = [psum(f"pp{i}") for i in range(4)]
    PS45 = enter(nc.psum_tensor("pp45", [128, 1024], F32))
    PS67 = enter(nc.psum_tensor("pp67", [128, 1024], F32))
    P = P03 + [PS45[:, 0:512], PS45[:, 512:1024], PS67[:, 0:512], PS67[:, 512:1024]]
    ATN = VTB[:, :, :].rearrange("p a b -> p (a b)")

    sems = {}

    def mksem(n):
        s = Sem(enter(nc.semaphore(n)))
        sems[n] = s
        return s

    dIO = mksem("dIO")
    dSH = mksem("dSH")
    dKV = mksem("dKV")
    dWS = mksem("dWS")
    dWG = mksem("dWG")
    tS = mksem("tS")
    vS = mksem("vS")
    cS = mksem("cS")
    pS = mksem("pS")

    script = []
    TE = Rec(script, "tensor")
    VE = Rec(script, "vector")
    SE = Rec(script, "scalar")
    GE = Rec(script, "gpsimd")
    SY = Rec(script, "sync")

    bguard = {}
    wfb_busy = {0: (tS, 0), 1: (tS, 0), 2: (tS, 0), 3: (tS, 0)}

    def pwait(p):
        g = bguard.pop(id(p), None)
        if g is not None:
            TE.wait(g[0], g[1])

    def pdone(p, sem):
        bguard[id(p)] = (sem, sem.v)

    GE.dma_start(out=PRM[:, :], in_=cf_h[:, 0:128], inc=(dWG, 16))
    GE.dma_start(out=IDB[:, :], in_=cf_h[:, 128:256], inc=(dWG, 16))
    GE.dma_start(out=IDN[:, :], in_=cf_h[:, 128:256], inc=(dWG, 16))
    GE.dma_start(out=ONEC[:, :], in_=cf_h[:, 256:257], allow_slow_non_contiguous=True, inc=(dWG, 16))
    GE.dma_start(out=ROWS[:, 0, :], in_=cf_h[0:1, 257:385], inc=(dWG, 16))
    GE.dma_start(out=ROWS[:, 1, :], in_=cf_h[0:1, 385:513], inc=(dWG, 16))
    GE.dma_start(out=ROWS[:, 2, :], in_=cf_h[0:1, 513:641], inc=(dWG, 16))
    GE.dma_start(out=UPS[:, :], in_=upsc_h[:, :], inc=(dWG, 16))
    GE.dma_start(out=DNS[:, :], in_=dnsc_h[:, :], inc=(dWG, 16))
    GE.dma_start(out=OSS[:, :], in_=osc_h[:, :], inc=(dWG, 16))
    GE.dma_start(out=MGC[:, :], in_=magic_h[:, :], inc=(dWG, 16))
    CONST_LD = dWG.v
    VE.memset(ZRO[:, :], 0)
    VE.memset(EPSC[:, :], EPS)
    VE.memset(NRM[:, 0, :], 0.0)
    VE.memset(NRM[:, 1, :], 0.0, inc=(vS, 1))

    TE.wait(dWG, CONST_LD)
    pwait(P[6])
    TE.matmul(P[6][:, 0:1], ROWS[:, 2, :], OSS[:, :], start=True, stop=True, inc=(tS, 1))
    VE.wait(tS)
    VE.tensor_copy(TD[:, 0:1], P[6][:, 0:1], inc=(vS, 1))
    pdone(P[6], vS)
    SE.wait(vS)
    SE.activation(CMB[:, :], DNS[:, :], AF.Copy, scale=TD[:, 0:1], inc=(cS, 1))
    CMB_DONE = cS.v

    def emit_ln(out_buf, osl, xready, outfree):
        VE.wait(*xready)
        SE.wait(*xready)
        lnmm = {}
        for dc in range(DC):
            b = dc % 2
            if dc >= 2:
                VE.wait(tS, lnmm[dc - 2])
                SE.wait(tS, lnmm[dc - 2])
            VE.tensor_copy(XBF[:, b, :], XB[:, dc, :], inc=(vS, 1))
            SE.activation(XSQ[:, b, :], XB[:, dc, :], AF.Square, inc=(cS, 1))
            TE.wait(vS)
            TE.wait(cS)
            if dc == 0:
                pwait(P[6])
                pwait(P[7])
            TE.matmul(P[6][0:1, :], ONEC[:, :], XBF[:, b, :], start=(dc == 0), stop=(dc == DC - 1))
            TE.matmul(P[7][0:1, :], ONEC[:, :], XSQ[:, b, :], start=(dc == 0), stop=(dc == DC - 1),
                      inc=(tS, 1))
            lnmm[dc] = tS.v
        VE.wait(tS)
        VE.tensor_scalar_mul(STAT[:, :, 0], P[6][0:1, :], 1.0 / D, inc=(vS, 1))
        VE.tensor_scalar_mul(STAT[:, :, 1], P[7][0:1, :], 1.0 / D, inc=(vS, 1))
        pdone(P[6], vS)
        pdone(P[7], vS)
        VE.wait(vS)
        VE.tensor_tensor(STAT[:, :, 2], STAT[:, :, 0], STAT[:, :, 0], ALU.mult, inc=(vS, 1))
        VE.wait(vS)
        VE.tensor_tensor(STAT[:, :, 1], STAT[:, :, 1], STAT[:, :, 2], ALU.subtract, inc=(vS, 1))
        SE.wait(vS)
        SE.activation(STAT[:, :, 1], STAT[:, :, 1], AF.Ln, bias=EPSC[0:1, :])
        SE.activation(STAT[:, :, 1], STAT[:, :, 1], AF.Exp, scale=-0.5, inc=(cS, 1))
        VE.wait(cS)
        VE.tensor_tensor(STAT[:, :, 2], STAT[:, :, 0], STAT[:, :, 1], ALU.mult, inc=(vS, 1))
        VE.wait(vS)
        TE.wait(vS)
        pwait(P[6])
        TE.matmul(P[6][:, :], ROWS[:, 0, :], STAT[:, :, 1], start=True, stop=True)
        pwait(P[7])
        TE.matmul(P[7][:, :], ROWS[:, 0, :], STAT[:, :, 2], start=True, stop=True, inc=(tS, 1))
        VE.wait(tS)
        VE.tensor_copy(RSB[:, :], P[6][:, :])
        VE.tensor_copy(MRSB[:, :], P[7][:, :], inc=(vS, 1))
        pdone(P[6], vS)
        pdone(P[7], vS)
        GE.wait(vS)
        if outfree is not None:
            GE.wait(*outfree)
        pol = {}
        for dc in range(DC):
            if dc > 1:
                VE.wait(pS, pol[dc - 2])
            VE.tensor_tensor(TF[:, dc % 2, :], XB[:, dc, :], RSB[:, :], ALU.mult, inc=(vS, 1))
            GE.wait(vS)
            GE.tensor_tensor(out_buf[:, dc, osl], TF[:, dc % 2, :], MRSB[:, :], ALU.subtract,
                             inc=(pS, 1))
            pol[dc] = pS.v
        return pS.v

    def hidsl(hc):
        if hc < 32:
            return KRB[:, hc // 2, (hc % 2) * TB:(hc % 2) * TB + TB]
        h2 = hc - 32
        return VTB[:, h2 // 4, (h2 % 4) * TB:(h2 % 4) * TB + TB]

    def emit_ffn(lnsl, ln_ready, wmode, hid_free, scr_gate=None, xdst=None):
        slab = {}
        amm = {}
        relu_done = {}
        sgn = {}
        scrw = {}
        NB = 4
        for step in range(HC + 1):
            if step < HC:
                hc = step
                b = hc % NB
                if wmode == "fp32":
                    if hc >= 1:
                        GE.wait(cS, sgn[hc - 1])
                    GE.dma_start(out=WF[:, :], in_=up_h[hc, :, :, :].rearrange("p a b -> p (a b)"), inc=(dWG, 16))
                    SE.wait(dWG)
                    SE.wait(*wfb_busy[b])
                    if hc >= NB:
                        SE.wait(dWG, scrw[hc - NB])
                    SE.activation(WFB[:, b, :], WF[:, :], AF.Sign, inc=(cS, 1))
                    sgn[hc] = cS.v
                    GE.wait(cS, sgn[hc])
                    GE.dma_start(out=upsgn_h[hc, :, :, :],
                                 in_=WFB[:, b, :].rearrange("p (a c) -> p a c", a=DC), inc=(dWG, 16))
                    scrw[hc] = dWG.v
                    slab[hc] = (cS, sgn[hc])
                else:
                    if hc == 0 and scr_gate is not None:
                        SY.wait(*scr_gate)
                    SY.wait(*wfb_busy[b])
                    SY.dma_start(out=WFB[:, b, :].rearrange("p (a c) -> p a c", a=DC),
                                 in_=upsgn_h[hc, :, :, :], inc=(dWS, 16))
                    slab[hc] = (dWS, dWS.v)
            if step >= 1:
                h = step - 1
                b = h % NB
                pb = P[h % 4]
                ssem, scnt = slab[h]
                TE.wait(ssem, scnt)
                if h == 0:
                    TE.wait(pS, ln_ready)
                    if hid_free is not None:
                        SE.wait(*hid_free)
                if h >= 4:
                    TE.wait(cS, relu_done[h - 4])
                else:
                    pwait(pb)
                for dc in range(DC):
                    TE.matmul(pb[:, :], WFB[:, b, dc * 128:dc * 128 + 128], LNB[:, dc, lnsl],
                              start=(dc == 0), stop=(dc == DC - 1),
                              inc=(tS, 1) if dc == DC - 1 else None)
                amm[h] = tS.v
                wfb_busy[b] = (tS, amm[h])
                SE.wait(tS, amm[h])
                SE.activation(hidsl(h), pb[:, :], AF.Relu, scale=UPS[:, h:h + 1], inc=(cS, 1))
                relu_done[h] = cS.v
                pdone(pb, cS)
        a_done = cS.v
        bmm = {}
        bsgn = {}
        bscrw = {}
        last_res = 0
        NB = 4
        for dcg in range(8):
            pwait(P[2])
            pwait(P[3])
            for hg8 in range(8):
                gi = dcg * 8 + hg8
                b = gi % NB
                if wmode == "fp32":
                    if gi >= 1:
                        GE.wait(cS, bsgn[gi - 1])
                    GE.dma_start(out=WF[:, :], in_=dn_h[dcg, :, hg8 * 8:hg8 * 8 + 8, :].rearrange("p a b -> p (a b)"),
                                 inc=(dWG, 16))
                    SE.wait(dWG)
                    SE.wait(*wfb_busy[b])
                    if gi >= NB:
                        SE.wait(dWG, bscrw[gi - NB])
                    SE.activation(WFB[:, b, :], WF[:, :], AF.Sign, inc=(cS, 1))
                    bsgn[gi] = cS.v
                    GE.wait(cS, bsgn[gi])
                    GE.dma_start(out=dnsgn_h[dcg, :, hg8 * 8:hg8 * 8 + 8, :],
                                 in_=WFB[:, b, :].rearrange("p (a c) -> p a c", a=8), inc=(dWG, 16))
                    bscrw[gi] = dWG.v
                    bload = (cS, bsgn[gi])
                else:
                    if gi == 0 and scr_gate is not None:
                        SY.wait(*scr_gate)
                    SY.wait(*wfb_busy[b])
                    SY.dma_start(out=WFB[:, b, :].rearrange("p (a c) -> p a c", a=8),
                                 in_=dnsgn_h[dcg, :, hg8 * 8:hg8 * 8 + 8, :], inc=(dWS, 16))
                    bload = (dWS, dWS.v)
                TE.wait(bload[0], bload[1])
                if gi == 0:
                    TE.wait(cS, a_done)
                for hloc in range(8):
                    hc = hg8 * 8 + hloc
                    st = (hg8 == 0 and hloc == 0)
                    sp = (hg8 == 7 and hloc == 7)
                    TE.matmul(P[2][:, :], WFB[:, b, hloc * 256:hloc * 256 + 128], hidsl(hc),
                              start=st, stop=sp)
                    TE.matmul(P[3][:, :], WFB[:, b, hloc * 256 + 128:hloc * 256 + 256], hidsl(hc),
                              start=st, stop=sp, inc=(tS, 1) if hloc == 7 else None)
                bmm[gi] = tS.v
                wfb_busy[b] = (tS, bmm[gi])
            SE.wait(tS, bmm[dcg * 8 + 7])
            SE.wait(cS, CMB_DONE)
            if dcg >= 1:
                SE.wait(vS, last_res)
            SE.activation(TD[:, :], P[2][:, :], AF.Copy, scale=CMB[:, 2 * dcg:2 * dcg + 1],
                          inc=(cS, 1))
            VE.wait(cS)
            VE.tensor_tensor(XB[:, 2 * dcg, :], XB[:, 2 * dcg, :], TD[:, :], ALU.add, inc=(vS, 1))
            SE.wait(vS)
            SE.activation(TD[:, :], P[3][:, :], AF.Copy, scale=CMB[:, 2 * dcg + 1:2 * dcg + 2],
                          inc=(cS, 1))
            pdone(P[2], cS)
            pdone(P[3], cS)
            VE.wait(cS)
            VE.tensor_tensor(XB[:, 2 * dcg + 1, :], XB[:, 2 * dcg + 1, :], TD[:, :], ALU.add,
                             inc=(vS, 1))
            last_res = vS.v
            if xdst is not None:
                SY.wait(vS, last_res)
                SY.dma_start(out=xdst[:, 2 * dcg:2 * dcg + 2, :],
                             in_=XB[:, 2 * dcg:2 * dcg + 2, :], inc=(dIO, 16))
        return vS.v

    def emit_proj(out_buf, osl, lnsl, ln_ready, wsrc, rope, qoff, cs_ready, scr_gate=None):
        wload = {}
        pmm = {}
        emarks = {}
        ropemm = {}
        rpend = []
        NB = 4
        for step in range(DC + 1):
            if step < DC:
                mc = step
                b = mc % NB
                if wsrc[0] == "cast":
                    GE.wait(*wfb_busy[b])
                    GE.dma_start(out=WFB[:, b, :].rearrange("p (a c) -> p a c", a=DC),
                                 in_=wsrc[1][mc, :, :, :], inc=(dWG, 16))
                    if wsrc[3]:
                        GE.wait(dWG)
                        GE.dma_start(out=wsrc[2][mc, :, :, :],
                                     in_=WFB[:, b, :].rearrange("p (a c) -> p a c", a=DC),
                                     inc=(dWG, 16))
                    wload[mc] = (dWG, dWG.v)
                else:
                    if mc == 0 and scr_gate is not None:
                        SY.wait(*scr_gate)
                    SY.wait(*wfb_busy[b])
                    SY.dma_start(out=WFB[:, b, :].rearrange("p (a c) -> p a c", a=DC),
                                 in_=wsrc[1][mc, :, :, :], inc=(dWS, 16))
                    wload[mc] = (dWS, dWS.v)
            if step >= 1:
                m = step - 1
                b = m % NB
                pb = P[m % 4]
                TE.wait(wload[m][0], wload[m][1])
                if m == 0:
                    TE.wait(pS, ln_ready)
                    if cs_ready is not None:
                        VE.wait(*cs_ready)
                        GE.wait(*cs_ready)
                if m >= 4:
                    TE.wait(cS, emarks[m - 4])
                else:
                    pwait(pb)
                for dc in range(DC):
                    TE.matmul(pb[:, :], WFB[:, b, dc * 128:dc * 128 + 128], LNB[:, dc, lnsl],
                              start=(dc == 0), stop=(dc == DC - 1),
                              inc=(tS, 1) if dc == DC - 1 else None)
                pmm[m] = tS.v
                wfb_busy[b] = (tS, pmm[m])
                if rope is None:
                    SE.wait(tS, pmm[m])
                    SE.activation(out_buf[:, m, osl], pb[:, :], AF.Copy, inc=(cS, 1))
                    emarks[m] = cS.v
                    pdone(pb, cS)
                else:
                    SE.wait(tS, pmm[m])
                    if m >= 1:
                        SE.wait(tS, ropemm.get(m - 1, 0))
                    SE.activation(KPRE2[:, m % 2, :], pb[:, :], AF.Copy, inc=(cS, 1))
                    emarks[m] = cS.v
                    pdone(pb, cS)
                    VE.wait(cS, emarks[m])
                    GE.wait(cS, emarks[m])
                    GE.tensor_tensor(T1s[:, m % 2, :], KPRE2[:, m % 2, :],
                                     CSB[:, 1, qoff:qoff + TB], ALU.mult, inc=(pS, 1))
                    VE.tensor_tensor(T2s[:, m % 2, :], KPRE2[:, m % 2, :],
                                     CSB[:, 0, qoff:qoff + TB], ALU.mult, inc=(vS, 1))
                    rpend.append((m, pS.v, vS.v))
                    if len(rpend) > 1 or step == DC:
                        m2, pv_, vv_ = rpend.pop(0)
                        TE.wait(pS, pv_)
                        TE.wait(vS, vv_)
                        pwait(P[4])
                        TE.matmul(P[4][:, :], PRM[:, :], T1s[:, m2 % 2, :], start=True, stop=False)
                        TE.matmul(P[4][:, :], IDB[:, :], T2s[:, m2 % 2, :], start=False, stop=True,
                                  inc=(tS, 1))
                        ropemm[m2] = tS.v
                        SE.wait(tS, ropemm[m2])
                        SE.activation(out_buf[:, m2, osl], P[4][:, :], AF.Copy, inc=(cS, 1))
                        pdone(P[4], cS)
        if rope is not None and rpend:
            m2, pv_, vv_ = rpend.pop(0)
            TE.wait(pS, pv_)
            TE.wait(vS, vv_)
            pwait(P[4])
            TE.matmul(P[4][:, :], PRM[:, :], T1s[:, m2 % 2, :], start=True, stop=False)
            TE.matmul(P[4][:, :], IDB[:, :], T2s[:, m2 % 2, :], start=False, stop=True,
                      inc=(tS, 1))
            ropemm[m2] = tS.v
            SE.wait(tS, ropemm[m2])
            SE.activation(out_buf[:, m2, osl], P[4][:, :], AF.Copy, inc=(cS, 1))
            pdone(P[4], cS)
        return cS.v, tS.v

    def emit_vproj(lnsl, ln_ready, wsrc, tb, scr_gate=None):
        wload = {}
        pmm = {}
        evd = {}
        trd = {}
        NB = 4
        for step in range(DC + 1):
            if step < DC:
                mc = step
                b = mc % NB
                if wsrc[0] == "cast":
                    GE.wait(*wfb_busy[b])
                    GE.dma_start(out=WFB[:, b, :].rearrange("p (a c) -> p a c", a=DC),
                                 in_=wsrc[1][mc, :, :, :], inc=(dWG, 16))
                    if wsrc[3]:
                        GE.wait(dWG)
                        GE.dma_start(out=wsrc[2][mc, :, :, :],
                                     in_=WFB[:, b, :].rearrange("p (a c) -> p a c", a=DC),
                                     inc=(dWG, 16))
                    wload[mc] = (dWG, dWG.v)
                else:
                    if mc == 0 and scr_gate is not None:
                        SY.wait(*scr_gate)
                    SY.wait(*wfb_busy[b])
                    SY.dma_start(out=WFB[:, b, :].rearrange("p (a c) -> p a c", a=DC),
                                 in_=wsrc[1][mc, :, :, :], inc=(dWS, 16))
                    wload[mc] = (dWS, dWS.v)
            if step >= 1:
                m = step - 1
                b = m % NB
                pb = P[m % 4]
                TE.wait(wload[m][0], wload[m][1])
                if m == 0:
                    TE.wait(pS, ln_ready)
                if m >= 4:
                    TE.wait(vS, evd[m - 4])
                else:
                    pwait(pb)
                for dc in range(DC):
                    TE.matmul(pb[:, :], WFB[:, b, dc * 128:dc * 128 + 128], LNB[:, dc, lnsl],
                              start=(dc == 0), stop=(dc == DC - 1),
                              inc=(tS, 1) if dc == DC - 1 else None)
                pmm[m] = tS.v
                wfb_busy[b] = (tS, pmm[m])
                VE.wait(tS, pmm[m])
                if m >= 1:
                    VE.wait(tS, trd[m - 1])
                VE.tensor_copy(TF[:, m % 2, :], pb[:, :], inc=(vS, 1))
                evd[m] = vS.v
                pdone(pb, vS)
                TE.wait(vS, evd[m])
                for i in range(4):
                    pt = P[4 + (i % 2)]
                    pwait(pt)
                    TE.transpose(pt[:, 0:128], TF[:, m % 2, i * 128:i * 128 + 128], IDN[:, :],
                                 inc=(tS, 1))
                    SE.wait(tS)
                    SE.activation(VTB[:, tb * 4 + i, m * 128:m * 128 + 128], pt[:, 0:128],
                                  AF.Copy, inc=(cS, 1))
                    pdone(pt, cS)
                trd[m] = tS.v
        return cS.v

    def emit_norms(buf, ncols, col, ready):
        SE.wait(*ready)
        for h in range(NH):
            SE.activation(KSQ[:, 0:ncols], buf[:, h, 0:ncols], AF.Square, inc=(cS, 1))
            TE.wait(cS)
            pwait(P[6])
            TE.matmul(P[6][0:1, :], ONEC[:, :], KSQ[:, 0:TB], start=True, stop=True)
            pwait(P[7])
            TE.matmul(P[7][0:1, :], ONEC[:, :], KSQ[:, TB:2 * TB], start=True, stop=True,
                      inc=(tS, 1))
            VE.wait(tS)
            VE.reduce_max(KNT[:, 0:1], P[6][0:1, :], axis=AX.X, inc=(vS, 1))
            VE.reduce_max(KNT[:, 1:2], P[7][0:1, :], axis=AX.X, inc=(vS, 1))
            pdone(P[6], vS)
            pdone(P[7], vS)
            VE.wait(vS)
            VE.tensor_tensor(KNT[:, 0:1], KNT[:, 0:1], KNT[:, 1:2], ALU.max, inc=(vS, 1))
            VE.wait(vS)
            VE.tensor_tensor(NRM[:, col, h:h + 1], NRM[:, col, h:h + 1], KNT[:, 0:1], ALU.max,
                             inc=(vS, 1))
        return vS.v

    def arm_info(arm):
        log = physical_to_logical_core_id(int(arm) % 128, arm.node_id_4,
                                          container_num_cores=8) % 8
        return log, log % 2

    regs = {}

    xb_free_v = 0
    xb_free_io = 0
    lnb_free = {0: (tS, 0), 1: (tS, 0)}
    hid_free = None
    ffn_scr = None
    k_scr = None
    v_scr = None
    TBSL = [slice(0, TB), slice(TB, 2 * TB)]

    for slot in range(COV):
        SY.wait(pS)
        SY.wait(vS)
        SY.dma_start(out=CSB[:, 0, :], in_=cscov_h[slot, 0, :, :], inc=(dIO, 16))
        SY.dma_start(out=CSB[:, 1, :], in_=cscov_h[slot, 1, :, :], inc=(dIO, 16))
        cs_ready = (dIO, dIO.v)
        for tb in range(2):
            wmode = "fp32" if (slot == 0 and tb == 0) else "bf16"
            SY.wait(vS, xb_free_v)
            SY.wait(dIO, xb_free_io)
            SY.dma_start(out=XB[:, :, :], in_=xcov_h[slot, :, :, TBSL[tb]], inc=(dIO, 16))
            xready = (dIO, dIO.v)
            lnr = emit_ln(LNB, TBSL[tb], xready, lnb_free[tb])
            res = emit_ffn(TBSL[tb], lnr, wmode, hid_free, scr_gate=ffn_scr,
                           xdst=xmid_h[slot, :, :, TBSL[tb]])
            if slot == 0 and tb == 0:
                ffn_scr = (dWG, dWG.v)
            hid_free = None
            lnb_free[tb] = (tS, tS.v)
            xb_free_v = res
            xb_free_io = dIO.v
        for tb in range(2):
            SY.wait(vS, xb_free_v)
            SY.wait(dIO, xb_free_io)
            SY.dma_start(out=XB[:, :, :], in_=xmid_h[slot, :, :, TBSL[tb]], inc=(dIO, 16))
            xready = (dIO, dIO.v)
            lnr = emit_ln(LNB, TBSL[tb], xready, lnb_free[tb])
            ksrc = ("cast", wk_h, wkbf_h, True) if (slot == 0 and tb == 0) else ("scr", wkbf_h)
            kdone, kmm = emit_proj(KRB, TBSL[tb], TBSL[tb], lnr, ksrc, "rope", tb * TB,
                                   cs_ready, scr_gate=k_scr)
            if slot == 0 and tb == 0:
                k_scr = (dWG, dWG.v)
            vsrc = ("cast", wv_h, wvbf_h, True) if (slot == 0 and tb == 0) else ("scr", wvbf_h)
            vdone = emit_vproj(TBSL[tb], lnr, vsrc, tb, scr_gate=v_scr)
            if slot == 0 and tb == 0:
                v_scr = (dWG, dWG.v)
            lnb_free[tb] = (tS, tS.v)
            xb_free_v = vS.v
        emit_norms(KRB, TPC, 0, (cS, cS.v))
        SY.wait(cS)
        SY.wait(vS)
        kvw = []
        for hq in range(4):
            dSH.v += 16
        for tt in range(8):
            dSH.v += 16
        pre = dSH.v

        def mk_kv_writer(slotc, target):
            def go(sy):
                for arm in nc.switch_core_id([nc.sync], n=128):
                    log, par = arm_info(arm)
                    g = 2 * slotc + par
                    for hq in range(4):
                        sy.dma_start(
                            out=shk_h[:, hq * 4:hq * 4 + 4, g * 1024:(g + 1) * 1024],
                            in_=KRB[:, hq * 4:hq * 4 + 4, :],
                        ).then_inc(target, 16)
                    for tt in range(8):
                        sy.dma_start(
                            out=shv_h[g * 1024 + tt * 128:g * 1024 + (tt + 1) * 128, :],
                            in_=VTB[:, tt, :],
                        ).then_inc(target, 16)
            return go

        SY.raw(mk_kv_writer(slot, dSH.h))
        hid_free = (dSH, dSH.v)

    knorm_ready = vS.v
    SY.wait(vS, knorm_ready)
    all_sh = dSH.v
    dSH.v += 16
    dSH.v += 16
    dSH.v += 16
    BAR_END = dSH.v

    def barrier(sy):
        r_mg = sy.register("r_mg").__enter__()
        r_pf = sy.register("r_pf").__enter__()
        r_nd = sy.register("r_nd").__enter__()
        regs["mg"], regs["pf"], regs["nd"] = r_mg, r_pf, r_nd
        for arm in nc.switch_core_id([nc.sync], n=128):
            log, par = arm_info(arm)
            other = 1 - par
            sy.dma_start(out=shkn_h[par:par + 1, :], in_=NRM[:, 0, :]).then_inc(dSH.h, 16)
            sy.wait_ge(dSH.h, all_sh + 16)
            sy.dma_start(out=shfl_h[par:par + 1, :], in_=MGC[:, :]).then_inc(dSH.h, 16)
            sy.wait_ge(dSH.h, all_sh + 32)
            sy.load(r_mg, MGC[0:1, 0:1])
            sy.reg_mov(r_nd, 1)
            with sy.While(r_nd):
                sy.load(r_pf, shfl_h[other:other + 1, 0:1])
                sy.reg_alu(r_nd, r_pf, r_mg, ALU.not_equal)
            sy.dma_start(out=PKN[:, :], in_=shkn_h[other:other + 1, :]).then_inc(dSH.h, 16)

    SY.raw(barrier)
    pkn_ready = BAR_END

    SY.wait(pS)
    SY.wait(vS)
    SY.dma_start(out=CSB[:, 0, :], in_=csown_h[0, :, :], inc=(dIO, 16))
    SY.dma_start(out=CSB[:, 1, :], in_=csown_h[1, :, :], inc=(dIO, 16))
    cs_own = (dIO, dIO.v)

    qdone = 0
    for tb in range(2):
        SY.wait(vS, xb_free_v)
        SY.wait(dIO, xb_free_io)
        dIO.v += 16

        def mk_xmid_rd(tbc, target, cnt):
            def go(sy):
                for arm in nc.switch_core_id([nc.sync], n=128):
                    log, par = arm_info(arm)
                    myslot = log // 2
                    sy.dma_start(out=XB[:, :, :],
                                 in_=xmid_h[myslot, :, :, TBSL[tbc]]).then_inc(target, 16)
                sy.wait_ge(target, cnt)
            return go

        SY.raw(mk_xmid_rd(tb, dIO.h, dIO.v))
        xready = (dIO, dIO.v)
        lnr = emit_ln(LNB, TBSL[tb], xready, lnb_free[tb])
        qsrc = ("cast", wq_h, None, False)
        qdone, _ = emit_proj(KRB, TBSL[tb], TBSL[tb], lnr, qsrc, "rope", tb * TB, cs_own)
        lnb_free[tb] = (tS, tS.v)
        xb_free_v = vS.v
    emit_norms(KRB, TPC, 1, (cS, qdone))

    VE.wait(dSH, pkn_ready)
    VE.wait(vS)
    VE.tensor_tensor(NRM[:, 2, :], NRM[:, 0, :], PKN[:, :], ALU.max, inc=(vS, 1))
    VE.wait(vS)
    VE.tensor_tensor(NRM[:, 2, :], NRM[:, 2, :], NRM[:, 1, :], ALU.mult, inc=(vS, 1))
    VE.wait(vS)
    VE.tensor_scalar_max(NRM[:, 2, :], NRM[:, 2, :], 1e-6, inc=(vS, 1))
    SE.wait(vS)
    SE.activation(NRM[:, 3, :], NRM[:, 2, :], AF.Sqrt, scale=1.0 / 128.0, inc=(cS, 1))
    VE.wait(cS)
    VE.tensor_scalar(NRM[:, 3, :], NRM[:, 3, :], 1.01, 0.5, ALU.mult, op1=ALU.add, inc=(vS, 1))
    TE.wait(vS)
    pwait(P[6])
    TE.matmul(P[6][:, 0:16], ROWS[:, 1, :], NRM[:, 3, :], start=True, stop=True, inc=(tS, 1))
    VE.wait(tS)
    VE.tensor_copy(NEGM[:, :], P[6][:, 0:16], inc=(vS, 1))
    pdone(P[6], vS)
    negm_ready = vS.v

    q_ready = qdone

    exp_done = {}
    pv_mm = {}
    acc_done = {}
    scr_free = {}
    atn_marks = []
    for qh in range(2):
        qsl = slice(qh * TB, qh * TB + TB)
        for (h0, h1) in HGROUPS:
            g = h1 - h0
            kready = {}
            vready = {}
            kcons = {}
            pvdone = {}

            def kload(j):
                if j >= 2:
                    SY.wait(tS, kcons[j - 2])
                for hi in range(g):
                    SY.dma_start(out=KSL[:, j % 2, hi, :],
                                 in_=shk_h[:, h0 + hi, j * 256:(j + 1) * 256],
                                 inc=(dKV, 16))
                kready[j] = dKV.v

            def vload(kc):
                if kc >= 4:
                    SY.wait(tS, pvdone[kc - 4])
                SY.dma_start(out=VSL[:, kc % 4, 0:g * 128],
                             in_=shv_h[kc * 128:(kc + 1) * 128, h0 * 128:h1 * 128],
                             inc=(dKV, 16))
                vready[kc] = dKV.v

            SY.wait(dSH, pkn_ready)
            kload(0)
            kload(1)
            vload(0)
            vload(1)
            vload(2)
            vload(3)
            pit = 0
            pending = []

            def flush_pv(keep):
                while len(pending) > keep:
                    hi2, j2, pit2 = pending.pop(0)
                    TE.wait(cS, exp_done[pit2])
                    a2 = 2 * (pit2 % 2)
                    for tt2 in range(2):
                        kc2 = 2 * j2 + tt2
                        TE.wait(dKV, vready[kc2])
                        if kc2 == 0:
                            pwait(P[hi2])
                        TE.matmul(P[hi2][:, :], VSL[:, kc2 % 4, hi2 * 128:(hi2 + 1) * 128],
                                  EXB[:, a2 + tt2, :], start=(kc2 == 0), stop=(kc2 == 63),
                                  inc=(tS, 1))
                        pv_mm[pit2] = tS.v
                        if hi2 == g - 1:
                            pvdone[kc2] = tS.v
                            if kc2 + 4 < 64:
                                vload(kc2 + 4)

            for j in range(32):
                TE.wait(dKV, kready[j])
                for hi in range(g):
                    h = h0 + hi
                    sb2 = PS45 if pit % 2 == 0 else PS67
                    if pit == 0:
                        pwait(P[4])
                        pwait(P[5])
                        TE.wait(cS, q_ready)
                        TE.wait(vS, negm_ready)
                    elif pit == 1:
                        pwait(P[6])
                        pwait(P[7])
                    else:
                        TE.wait(cS, exp_done[pit - 2])
                    TE.matmul(sb2[:, 0:512], KSL[:, j % 2, hi, 0:128],
                              KRB[:, h, qsl], start=True, stop=True)
                    TE.matmul(sb2[:, 512:1024], KSL[:, j % 2, hi, 128:256],
                              KRB[:, h, qsl], start=True, stop=True, inc=(tS, 1))
                    smm = tS.v
                    flush_pv(1)
                    SE.wait(tS, smm)
                    if pit >= 4:
                        SE.wait(vS, acc_done[pit - 4])
                    a = 2 * (pit % 2)
                    SE.activation(EXB[:, a:a + 2, :].rearrange("p x t -> p (x t)"), sb2[:, :],
                                  AF.Exp, bias=NEGM[:, h:h + 1], scale=INVSQHD, inc=(cS, 1))
                    exp_done[pit] = cS.v
                    VE.wait(cS, exp_done[pit])
                    if j == 0:
                        VE.tensor_copy(ACC[:, hi, :], EXB[:, a, :])
                    else:
                        VE.tensor_tensor(ACC[:, hi, :], ACC[:, hi, :], EXB[:, a, :], ALU.add)
                    VE.tensor_tensor(ACC[:, hi, :], ACC[:, hi, :], EXB[:, a + 1, :], ALU.add,
                                     inc=(vS, 1))
                    acc_done[pit] = vS.v
                    pending.append((hi, j, pit))
                    pit += 1
                kcons[j] = tS.v
                if j + 2 < 32:
                    kload(j + 2)
            flush_pv(0)
            pdone(P[4], cS)
            pdone(P[5], cS)
            pdone(P[6], cS)
            pdone(P[7], cS)
            last_pv = tS.v
            for hi in range(g):
                h = h0 + hi
                VE.wait(tS, last_pv)
                VE.tensor_copy(T1, ACC[:, hi, :], inc=(vS, 1))
                TE.wait(vS)
                pwait(P[6])
                TE.matmul(P[6][0:1, :], ONEC[:, :], T1, start=True, stop=True, inc=(tS, 1))
                VE.wait(tS)
                VE.reciprocal(STAT[:, :, 0], P[6][0:1, :], inc=(vS, 1))
                pdone(P[6], vS)
                TE.wait(vS)
                pwait(P[7])
                TE.matmul(P[7][:, :], ROWS[:, 0, :], STAT[:, :, 0], start=True, stop=True,
                          inc=(tS, 1))
                VE.wait(tS)
                VE.tensor_copy(RSB[:, :], P[7][:, :], inc=(vS, 1))
                pdone(P[7], vS)
                VE.tensor_tensor(ATN[:, h * 1024 + qh * TB:h * 1024 + qh * TB + TB],
                                 P[hi][:, :], RSB[:, :], ALU.mult, inc=(vS, 1))
                pdone(P[hi], vS)
            atn_marks.append(vS.v)
    atn_ready = vS.v

    for tb in range(2):
        SY.wait(vS, xb_free_v)
        SY.wait(dIO, xb_free_io)
        dIO.v += 16
        SY.raw(mk_xmid_rd(tb, dIO.h, dIO.v))
        xready = (dIO, dIO.v)
        owload = {}
        omm = {}
        oev = {}
        NB = 4
        for step in range(DC + 1):
            if step < DC:
                mc = step
                b = mc % NB
                GE.wait(*wfb_busy[b])
                GE.dma_start(out=WFB[:, b, :].rearrange("p (a c) -> p a c", a=DC),
                             in_=wo_h[mc, :, :, :], inc=(dWG, 16))
                owload[mc] = (dWG, dWG.v)
            if step >= 1:
                m = step - 1
                b = m % NB
                pb = P[m % 4]
                TE.wait(owload[m][0], owload[m][1])
                if m == 0:
                    TE.wait(vS, atn_ready)
                if m >= 4:
                    TE.wait(vS, oev[m - 4])
                else:
                    pwait(pb)
                for dc in range(DC):
                    TE.matmul(pb[:, :], WFB[:, b, dc * 128:dc * 128 + 128],
                              ATN[:, dc * 1024 + tb * TB:dc * 1024 + tb * TB + TB],
                              start=(dc == 0), stop=(dc == DC - 1),
                              inc=(tS, 1) if dc == DC - 1 else None)
                omm[m] = tS.v
                wfb_busy[b] = (tS, omm[m])
                VE.wait(tS, omm[m])
                if m == 0:
                    VE.wait(*xready)
                VE.tensor_tensor(XB[:, m, :], XB[:, m, :], pb[:, :], ALU.add, inc=(vS, 1))
                oev[m] = vS.v
                pdone(pb, vS)
        SY.wait(vS, oev[DC - 1])
        SY.dma_start(out=x1_h[:, :, TBSL[tb]], in_=XB[:, :, :], inc=(dIO, 16))
        xb_free_v = vS.v
        xb_free_io = dIO.v
    o_done = tS.v

    hid_free = (tS, o_done)
    for tb in range(2):
        SY.wait(vS, xb_free_v)
        SY.wait(dIO, xb_free_io)
        SY.dma_start(out=XB[:, :, :], in_=x1_h[:, :, TBSL[tb]], inc=(dIO, 16))
        xready = (dIO, dIO.v)
        lnr = emit_ln(LNB, TBSL[tb], xready, lnb_free[tb])
        res = emit_ffn(TBSL[tb], lnr, "bf16", hid_free, scr_gate=ffn_scr)
        hid_free = None
        lnb_free[tb] = (tS, tS.v)
        SY.wait(vS, res)
        SY.dma_start(out=out_h[:, :, TBSL[tb]], in_=XB[:, :, :], inc=(dIO, 16))
        xb_free_v = res
        xb_free_io = dIO.v

    dIO.v += 16

    def resetter(sy):
        for arm in nc.switch_core_id([nc.sync], n=128):
            log, par = arm_info(arm)
            sy.dma_start(out=shfl_h[par:par + 1, :], in_=ZRO[:, :]).then_inc(dIO.h, 16)

    SY.raw(resetter)
    SY.wait(dIO)
    SY.wait(dSH)
    SY.wait(dKV)

    with nc.Block() as block:
        def replay(engine_name):
            def body(eng):
                for (en, op, a, k, inc) in script:
                    if en != engine_name:
                        continue
                    if op is None:
                        a[0](eng)
                        continue
                    inst = getattr(eng, op)(*a, **k)
                    if inc is not None:
                        inst.then_inc(inc[0], inc[1])
            return body

        block.tensor(replay("tensor"))
        block.vector(replay("vector"))
        block.scalar(replay("scalar"))
        block.gpsimd(replay("gpsimd"))
        block.sync(replay("sync"))

    for t in reversed(ctx):
        t.__exit__(None, None, None)
    return nc


_CACHE = {}


def _stage_weights(L0_attn_wq, L0_attn_wk, L0_attn_wv, L0_attn_wo,
                   L0_ffn_up_weight, L0_ffn_down_weight,
                   L0_ffn_up_scale, L0_ffn_down_scale, L0_ffn_output_scale):
    st = {}

    def wstage(w):
        return np.ascontiguousarray(w.reshape(DC, 128, DC, 128).transpose(0, 3, 2, 1))

    st["wqt"] = wstage(L0_attn_wq)
    st["wkt"] = wstage(L0_attn_wk)
    st["wvt"] = wstage(L0_attn_wv)
    st["wot"] = wstage(L0_attn_wo)
    st["upt"] = np.ascontiguousarray(
        L0_ffn_up_weight.reshape(HC, 128, DC, 128).transpose(0, 3, 2, 1))
    dw = L0_ffn_down_weight.reshape(8, 2, 128, HC, 128)
    st["dnt"] = np.ascontiguousarray(
        dw.transpose(0, 4, 3, 1, 2).reshape(8, 128, HC, 256))
    st["upsc"] = np.ascontiguousarray(L0_ffn_up_scale.reshape(HC, 128).T)
    st["dnsc"] = np.ascontiguousarray(L0_ffn_down_scale.reshape(DC, 128).T)
    st["osc"] = np.ascontiguousarray(L0_ffn_output_scale.reshape(1, 1))
    return st


def _consts():
    cf = np.zeros((128, 648), np.float32)
    prm = np.zeros((128, 128), np.float32)
    for i in range(64):
        prm[2 * i + 1, 2 * i] = 1.0
        prm[2 * i, 2 * i + 1] = 1.0
    cf[:, 0:128] = prm
    cf[:, 128:256] = np.eye(128, dtype=np.float32)
    cf[:, 256] = 1.0
    cf[0, 257:385] = 1.0
    cf[0, 385:513] = -1.0
    cf[0, 513:641] = 0.5
    return cf


def _cs_tables():
    import ml_dtypes
    hd = 128
    scale = np.arange(0, hd, 2, dtype=np.float64) / hd
    inv = 1.0 / 10000.0 ** scale
    t = np.arange(SEQ, dtype=np.float64)
    ang = np.outer(inv, t)
    C = np.empty((128, SEQ), np.float64)
    S2 = np.empty((128, SEQ), np.float64)
    C[0::2] = np.cos(ang)
    C[1::2] = np.cos(ang)
    S2[0::2] = np.sin(ang)
    S2[1::2] = -np.sin(ang)
    return (C.astype(ml_dtypes.bfloat16), S2.astype(ml_dtypes.bfloat16))


def kernel(L0_attn_x, L0_attn_wq, L0_attn_wk, L0_attn_wv, L0_attn_wo,
           L0_ffn_up_weight, L0_ffn_down_weight, L0_ffn_up_scale,
           L0_ffn_down_scale, L0_ffn_output_scale):
    import ml_dtypes
    if "nc" not in _CACHE:
        _CACHE["nc"] = build_program()
    nc = _CACHE["nc"]

    st = _stage_weights(np.asarray(L0_attn_wq, np.float32), np.asarray(L0_attn_wk, np.float32),
                        np.asarray(L0_attn_wv, np.float32), np.asarray(L0_attn_wo, np.float32),
                        np.asarray(L0_ffn_up_weight, np.float32),
                        np.asarray(L0_ffn_down_weight, np.float32),
                        np.asarray(L0_ffn_up_scale, np.float32),
                        np.asarray(L0_ffn_down_scale, np.float32),
                        np.asarray(L0_ffn_output_scale, np.float32))
    cf = _consts()
    C, S2 = _cs_tables()
    x = np.asarray(L0_attn_x, np.float32).reshape(SEQ, D)
    magic = np.random.randint(1, 2 ** 31 - 1, size=(1, NMAGIC)).astype(np.int32)

    def fm(chunk):
        blk = x[chunk * 1024:(chunk + 1) * 1024, :]
        return np.ascontiguousarray(blk.T.reshape(DC, 128, 1024).transpose(1, 0, 2))

    in_maps = []
    for c in range(8):
        par = c % 2
        chunks = [par, par + 2, par + 4, par + 6]
        xcov = np.stack([fm(g) for g in chunks], axis=0)
        cscov = np.stack(
            [np.stack([C[:, g * 1024:(g + 1) * 1024], S2[:, g * 1024:(g + 1) * 1024]], axis=0)
             for g in chunks], axis=0)
        csown = np.stack([C[:, c * 1024:(c + 1) * 1024], S2[:, c * 1024:(c + 1) * 1024]], axis=0)
        m = dict(st)
        m["xcov"] = xcov
        m["cscov"] = np.ascontiguousarray(cscov)
        m["csown"] = np.ascontiguousarray(csown)
        m["constf"] = cf
        m["magic"] = magic
        in_maps.append(m)

    res = run_bass_kernel_spmd(nc, in_maps, core_ids=list(range(8)))
    out = np.empty((SEQ, D), np.float32)
    for c in range(8):
        arr = res.results[c]["out"]
        out[c * 1024:(c + 1) * 1024, :] = arr.transpose(1, 0, 2).reshape(D, 1024).T
    return out.reshape(1, SEQ, D)
